# version 03
# baseline (speedup 1.0000x reference)
import numpy as np
from contextlib import ExitStack

import concourse.bass as bass
import concourse.mybir as mybir
from concourse.bass_utils import run_bass_kernel_spmd
from concourse.expressions_rust import RuntimeValue

F32 = mybir.dt.float32
BF16 = mybir.dt.bfloat16
I32 = mybir.dt.int32
AF = mybir.ActivationFunctionType
ALU = mybir.AluOpType
AX = mybir.AxisListType

D = 1024
T = 4096
NB = 4
TC = 512
NCH = 8
NS = 16
DHID = 4096
EPS = 1e-5
POLL_CAP = 20_000_000
PHASE_LIMIT = 99
DEBUG_DUMP = None
DEBUG_ATTN = False
LAST_RESULTS = None


class Sem:
    def __init__(self, handle):
        self.h = handle
        self.v = 0

    def inc(self, instr, n=1):
        instr.then_inc(self.h, n)
        self.v += n
        return self.v


def build_program():
    nc = bass.Bass("TRN2", target_bir_lowering=False)
    es = ExitStack()
    ap = lambda t: t.ap()

    x0_in = ap(nc.dram_tensor("x0", [NB, T, D], F32, kind="ExternalInput"))
    W_in = ap(nc.dram_tensor("W_in", [D, 4096], F32, kind="ExternalInput"))
    conv_w = ap(nc.dram_tensor("conv_w", [2, 128, 32], F32, kind="ExternalInput"))
    conv_b = ap(nc.dram_tensor("conv_b", [2, 128, 8], F32, kind="ExternalInput"))
    W_x = ap(nc.dram_tensor("W_x", [2048, 96], F32, kind="ExternalInput"))
    W_dt = ap(nc.dram_tensor("W_dt", [64, 2048], F32, kind="ExternalInput"))
    b_dt = ap(nc.dram_tensor("b_dt", [2, 128, 8], F32, kind="ExternalInput"))
    A_log = ap(nc.dram_tensor("A_log", [2, 128, 128], F32, kind="ExternalInput"))
    D_vec = ap(nc.dram_tensor("D_vec", [2, 128, 8], F32, kind="ExternalInput"))
    W_out = ap(nc.dram_tensor("W_out", [2048, D], F32, kind="ExternalInput"))
    W_spinT = ap(nc.dram_tensor("W_spinT", [D, D], F32, kind="ExternalInput"))
    b_spin = ap(nc.dram_tensor("b_spin", [128, 8], F32, kind="ExternalInput"))
    W_globT = ap(nc.dram_tensor("W_globT", [D, D], F32, kind="ExternalInput"))
    W_JT = ap(nc.dram_tensor("W_JT", [D, D], F32, kind="ExternalInput"))
    beta_in = ap(nc.dram_tensor("beta", [1, 1], F32, kind="ExternalInput"))
    w1T = ap(nc.dram_tensor("w1T", [D, DHID], F32, kind="ExternalInput"))
    w1s = ap(nc.dram_tensor("w1s", [128, 32], F32, kind="ExternalInput"))
    w2T = ap(nc.dram_tensor("w2T", [D, DHID], F32, kind="ExternalInput"))
    w2s = ap(nc.dram_tensor("w2s", [128, 32], F32, kind="ExternalInput"))
    h1T = ap(nc.dram_tensor("h1T", [DHID, DHID], F32, kind="ExternalInput"))
    h1s = ap(nc.dram_tensor("h1s", [128, 32], F32, kind="ExternalInput"))
    h2T = ap(nc.dram_tensor("h2T", [DHID, DHID], F32, kind="ExternalInput"))
    h2s = ap(nc.dram_tensor("h2s", [128, 32], F32, kind="ExternalInput"))
    w3T = ap(nc.dram_tensor("w3T", [DHID, D], F32, kind="ExternalInput"))
    w3s = ap(nc.dram_tensor("w3s", [128, 8], F32, kind="ExternalInput"))
    eye_in = ap(nc.dram_tensor("eye", [128, 128], F32, kind="ExternalInput"))
    const_in = ap(nc.dram_tensor("constv", [128, 8], F32,
                                 kind="ExternalInput"))

    out_tok = ap(nc.dram_tensor("out_tok", [2048, D], F32, kind="ExternalOutput"))
    out_id = ap(nc.dram_tensor("out_id", [1, 4], I32, kind="ExternalOutput"))

    sh_proj = ap(nc.dram_tensor("sh_proj", [2, NCH, 96, TC], F32,
                                addr_space="Shared"))
    sh_pout = ap(nc.dram_tensor("sh_pout", [2, NCH, D, TC], F32,
                                addr_space="Shared"))
    sh_s = ap(nc.dram_tensor("sh_s", [2, D, 2048], BF16, addr_space="Shared"))
    sh_flag = ap(nc.dram_tensor("sh_flag", [2, 32, 512], I32,
                                addr_space="Shared"))
    sh_mean = ap(nc.dram_tensor("sh_mean", [2, 128, 8], F32,
                                addr_space="Shared"))
    dr_x2 = ap(nc.dram_tensor("dr_x2", [128, 16384], F32))
    dr_ob = ap(nc.dram_tensor("dr_ob", [128, 16384], F32))
    dr_dbg = ap(nc.dram_tensor("dr_dbg", [128, 16384], F32))

    def SEM(name):
        return Sem(es.enter_context(nc.semaphore(name)))

    s_ld = SEM("s_ld")
    s_wld = SEM("s_wld")
    s_mm = SEM("s_mm")
    s_ev = SEM("s_ev")
    s_sc = SEM("s_sc")
    s_gp = SEM("s_gp")
    s_pl = SEM("s_pl")
    s_st = SEM("s_st")
    s_x0 = SEM("s_x0")
    s_xch = SEM("s_xch")
    s_wA = SEM("s_wA")
    s_wB = SEM("s_wB")
    s_xld = SEM("s_xld")
    s_ob = SEM("s_ob")
    s_tA = SEM("s_tA")
    s_tB = SEM("s_tB")
    s_sel = SEM("s_sel")
    s_stg = SEM("s_stg")

    def SB(name, shape, dt):
        return es.enter_context(nc.sbuf_tensor(name, shape, dt))

    def PSUM(name, shape, dt=F32):
        return es.enter_context(nc.psum_tensor(name, shape, dt))

    open_scopes = []

    class Scope:
        def __init__(self):
            self.es = ExitStack()
            open_scopes.append(self)

        def SB(self, name, shape, dt, side="left"):
            return self.es.enter_context(
                nc.sbuf_tensor(name, shape, dt, side=side))

        def PSUM(self, name, shape, dt=F32):
            return self.es.enter_context(nc.psum_tensor(name, shape, dt))

        def close(self):
            if self in open_scopes:
                open_scopes.remove(self)
            self.es.close()

    onesr_b = SB("onesr_b", [1, 128], BF16)
    onesr_f = SB("onesr_f", [1, 128], F32)
    onesc_f = SB("onesc_f", [128, 1], F32)
    s_eye = SB("s_eye", [128, 128], F32)
    s_eye_b = SB("s_eye_b", [128, 128], BF16)
    sid = SB("sid", [1, 4], I32)
    sflag1 = SB("sflag1", [1, 1], I32)
    sfpoll = SB("sfpoll", [1, 1], I32)
    hstate = SB("hstate", [128, 128], F32)
    sA = SB("sA", [128, 128], F32)
    sconvw = SB("sconvw", [128, 32], F32)
    sconvb = SB("sconvb", [128, 8], F32)
    sbdt = SB("sbdt", [128, 8], F32)
    sDv = SB("sDv", [128, 8], F32)
    sw1s = SB("sw1s", [128, 32], F32)
    sw2s = SB("sw2s", [128, 32], F32)
    sh1s = SB("sh1s", [128, 32], F32)
    sh2s = SB("sh2s", [128, 32], F32)
    sw3s = SB("sw3s", [128, 8], F32)
    sbspin = SB("sbspin", [128, 8], F32)
    sbeta = SB("sbeta", [128, 1], F32)
    r_mean = SB("r_mean", [1, TC], F32)
    r_var = SB("r_var", [1, TC], F32)
    r_rstd = SB("r_rstd", [1, TC], F32)
    r_mr = SB("r_mr", [1, TC], F32)
    sbc = SB("sbc", [128, 2 * TC], F32)
    ffv = SB("ffv", [1, 1], F32)
    ffs = SB("ffs", [1, 1], F32)
    ffg = SB("ffg", [1, 1], F32)
    scrA = SB("scrA", [128, 4096], F32)
    sln = SB("sln", [128, 4096], BF16)
    sconst = SB("sconst", [128, 8], F32)
    smean = SB("smean", [128, 8], F32)
    smean_p = SB("smean_p", [128, 8], F32)
    smean_b = SB("smean_b", [128, 8], BF16)
    sglob = SB("sglob", [128, 8], F32)

    pA = PSUM("pA", [128, 512])
    pB = PSUM("pB", [128, 512])
    pC = PSUM("pC", [128, 512])
    pD = PSUM("pD", [128, 512])
    pE = PSUM("pE", [128, 512])
    pF = PSUM("pF", [128, 512])

    CONSTS = [EPS, 1.0, -0.5, -1.0, 1.0 / 32.0, 0.5, 0.0, 2.0]
    for i, cv in enumerate(CONSTS):
        nc.const_aps.register(F32, cv, sconst[:, i:i + 1])

    block = es.enter_context(nc.Block())
    regs = {}

    sync_q = []

    def dsync(fn):
        sync_q.append(fn)

    def flush_sync():
        if not sync_q:
            return
        q = list(sync_q)
        sync_q.clear()

        @block.sync
        def _(s, q=q):
            idx = RuntimeValue(regs["ridx"], 0, 7)
            for i in nc.Switch(engines=[nc.sync], index=[idx], n=8):
                bsel, par = i >> 1, i & 1
                for fn in q:
                    fn(s, bsel, par)

    bank_free = {}

    def bwait(t, bank):
        info = bank_free.get(id(bank))
        if info:
            t.wait_ge(info[0].h, info[1])

    def bdone(bank, sem, val):
        bank_free[id(bank)] = (sem, val)

    def barrier():
        @block.vector
        def _(v):
            s_ev.inc(v.memset(ffv[:, :], 0.0))

        @block.scalar
        def _(sc):
            s_sc.inc(sc.copy(ffs[:, :], sconst[0:1, 0:1]))

        @block.gpsimd
        def _(g):
            s_gp.inc(g.memset(ffg[:, :], 0.0))

        waits = [(s_ld, s_ld.v), (s_wld, s_wld.v), (s_ev, s_ev.v),
                 (s_sc, s_sc.v), (s_gp, s_gp.v), (s_mm, s_mm.v),
                 (s_x0, s_x0.v), (s_xch, s_xch.v), (s_wA, s_wA.v),
                 (s_wB, s_wB.v), (s_xld, s_xld.v), (s_ob, s_ob.v),
                 (s_tA, s_tA.v), (s_tB, s_tB.v), (s_sel, s_sel.v),
                 (s_stg, s_stg.v),
                 (s_st, s_st.v)]

        for name in ["sync", "gpsimd", "vector", "scalar", "tensor"]:
            @getattr(block, name)
            def _(e, waits=waits):
                for sem, val in waits:
                    if val > 0:
                        e.wait_ge(sem.h, val)

    setup = Scope()
    wstage = setup.SB("wstage", [128, 8192], F32, side="right")

    scA = Scope()
    sWinX = scA.SB("sWinX", [128, 8192], BF16)
    sWinZ = scA.SB("sWinZ", [128, 8192], BF16)
    sWxT = scA.SB("sWxT", [128, 8 * 96], BF16)
    sWdt = scA.SB("sWdt", [64, 1024], BF16)
    sWout = scA.SB("sWout", [128, 8192], BF16)

    @block.sync
    def _(s):
        for nm in ["rphys", "ridx", "rflag", "rcond", "riter"]:
            regs[nm] = s.alloc_register(nm)
        rphys, ridx = regs["rphys"], regs["ridx"]
        for arm in nc.switch_core_id(engines=[nc.sync], n=128):
            s.reg_mov(rphys, int(arm))
        s.reg_alu(ridx, rphys, 7, ALU.bitwise_and)
        s.reg_save(sid[0:1, 2:3], rphys)
        s.reg_save(sid[0:1, 3:4], 7)
        s_ld.inc(s.dma_start(out=s_eye[:, :], in_=eye_in[:, :]), 16)
        s_ld.inc(s.dma_start(out=sconst[:, :], in_=const_in[:, :]), 16)
        for srcw, dst in [(w1s, sw1s), (w2s, sw2s), (h1s, sh1s), (h2s, sh2s),
                          (w3s, sw3s), (b_spin, sbspin)]:
            s_ld.inc(s.dma_start(out=dst[:, :], in_=srcw[:, :]), 16)
        s_ld.inc(s.dma_start(out=sbeta[0:1, 0:1], in_=beta_in[:, :]), 16)

    def _param_loads(s, bsel, par):
        s.reg_save(sid[0:1, 0:1], bsel)
        s.reg_save(sid[0:1, 1:2], par)
        s.dma_start(out=sA[:, :], in_=A_log[par, :, :]).then_inc(s_ld.h, 16)
        s.dma_start(out=sconvw[:, :],
                    in_=conv_w[par, :, :]).then_inc(s_ld.h, 16)
        for vec, dst in [(conv_b, sconvb), (b_dt, sbdt), (D_vec, sDv)]:
            s.dma_start(out=dst[:, :],
                        in_=vec[par, :, :]).then_inc(s_ld.h, 16)
    dsync(_param_loads)
    s_ld.v += 16 * 5
    PARAM_LD = s_ld.v

    @block.vector
    def _(v):
        v.memset(onesr_b[:, :], 1.0)
        v.memset(onesr_f[:, :], 1.0)
        v.memset(onesc_f[:, :], 1.0)
        v.memset(hstate[:, :], 0.0)
        s_ev.inc(v.memset(sflag1[:, :], 1))
    MEMSET_EV = s_ev.v

    def stage(load_fn, cast_fn, ndma):
        prev = s_ev.v

        def _ld(s, bsel, par, prev=prev, load_fn=load_fn):
            if prev > MEMSET_EV:
                s.wait_ge(s_ev.h, prev)
            load_fn(s, par)
        dsync(_ld)
        s_stg.v += 16 * ndma
        tgt = s_stg.v

        @block.vector
        def _(v, tgt=tgt):
            v.wait_ge(s_stg.h, tgt)
            s_ev.inc(cast_fn(v))

    def ld_winx(s, par):
        s.dma_start(
            out=wstage.ap().rearrange("p (sig m q) -> p sig m q", sig=8, m=8),
            in_=W_in.rearrange("(sig p) (h m q) -> h p sig m q", sig=8, h=4,
                               m=8)[par, :, :, :, :]).then_inc(s_stg.h, 16)

    def ld_winz(s, par):
        s.dma_start(
            out=wstage.ap().rearrange("p (sig m q) -> p sig m q", sig=8, m=8),
            in_=W_in.rearrange("(sig p) (h m q) -> h p sig m q", sig=8, h=4,
                               m=8)[par + 2, :, :, :, :]).then_inc(s_stg.h, 16)

    def ld_wout(s, par):
        s.dma_start(
            out=wstage.ap().rearrange("p (mu m q) -> p mu m q", mu=8, m=8),
            in_=W_out.rearrange("(h mu p) (m q) -> h p mu m q", h=2, mu=8,
                                m=8)[par, :, :, :, :]).then_inc(s_stg.h, 16)

    def ld_wx_wdt(s, par):
        s.dma_start(
            out=wstage.ap().rearrange("p (mu j) -> p mu j", mu=8)[:, :, 0:96],
            in_=W_x.rearrange("(h mu p) j -> h p mu j", h=2, mu=8)[
                par, :, :, :]).then_inc(s_stg.h, 16)
        s.dma_start(
            out=wstage.ap().rearrange("p (mu j) -> p mu j", mu=8)[
                0:64, :, 128:256],
            in_=W_dt.rearrange("r (h mu q) -> h r mu q", h=2, mu=8)[
                par, :, :, :]).then_inc(s_stg.h, 16)

    stage(ld_winx, lambda v: v.tensor_copy(out=sWinX[:, :], in_=wstage[:, :]),
          1)
    stage(ld_winz, lambda v: v.tensor_copy(out=sWinZ[:, :], in_=wstage[:, :]),
          1)
    stage(ld_wout, lambda v: v.tensor_copy(out=sWout[:, :], in_=wstage[:, :]),
          1)

    def cast_wx_wdt(v):
        v.tensor_copy(out=sWxT.ap().rearrange("p (mu j) -> p mu j", mu=8),
                      in_=wstage.ap().rearrange("p (mu j) -> p mu j", mu=8)[
                          :, :, 0:96])
        return v.tensor_copy(
            out=sWdt.ap().rearrange("p (mu q) -> p mu q", mu=8),
            in_=wstage.ap().rearrange("p (mu j) -> p mu j", mu=8)[
                0:64, :, 128:256])

    stage(ld_wx_wdt, cast_wx_wdt, 2)

    @block.vector
    def _(v, tgt=PARAM_LD):
        v.wait_ge(s_ld.h, tgt)
        s_ev.inc(v.tensor_copy(out=s_eye_b[:, :], in_=s_eye[:, :]))

    @block.scalar
    def _(sc, tgt=PARAM_LD):
        sc.wait_ge(s_ld.h, tgt)
        s_sc.inc(sc.activation(out=sA[:, :], in_=sA[:, :], func=AF.Exp))
    A_SC = s_sc.v

    @block.vector
    def _(v, tgt=A_SC):
        v.wait_ge(s_sc.h, tgt)
        s_ev.inc(v.tensor_scalar_mul(sA[:, :], sA[:, :], -1.0))

    @block.tensor
    def _(t, a=PARAM_LD, b=MEMSET_EV):
        t.wait_ge(s_ld.h, a)
        t.wait_ge(s_ev.h, b)
        s_mm.inc(t.matmul(pA[:, 0:1], onesr_f[:, :], sbeta[0:1, 0:1],
                          start=True, stop=True))
    BETA_MM = s_mm.v

    @block.vector
    def _(v, tgt=BETA_MM):
        v.wait_ge(s_mm.h, tgt)
        s_ev.inc(v.tensor_copy(out=sbeta[:, :], in_=pA[:, 0:1]))
    bdone(pA, s_ev, s_ev.v)

    setup.close()

    x0tm = scA.SB("x0tm", [128, 1024], F32)
    x0fm = scA.SB("x0fm", [128, 4096], F32)
    xcat = [scA.SB("xcat0", [128, 8 * 515], BF16)]
    ctx3 = scA.SB("ctx3", [128, 24], BF16)
    sxs = scA.SB("sxs", [128, 4096], BF16)
    sdt = scA.SB("sdt", [128, 4096], BF16)
    sdtx = scA.SB("sdtx", [128, 4096], BF16)
    szs = scA.SB("szs", [128, 4096], BF16)
    sproj = scA.SB("sproj", [96, TC], F32)
    sBC32 = scA.SB("sBC32", [32, TC], BF16)
    sel = scA.SB("sel", [32, 32 * 128], BF16)
    sBb = scA.SB("sBb", [128, TC], BF16)
    sCb = scA.SB("sCb", [128, TC], BF16)
    sa_t = [scA.SB(f"sa_t{i}", [128, TC], F32) for i in range(2)]
    su_t = [scA.SB(f"su_t{i}", [128, TC], F32) for i in range(2)]
    shh = [scA.SB(f"shh{i}", [128, TC], F32) for i in range(2)]
    syp = scA.SB("syp", [128, TC], F32)
    pRa_A = scA.PSUM("pRa_A", [1, 512])
    pRb_A = scA.PSUM("pRb_A", [1, 512])

    def debug_finish():
        flush_sync()

        @block.sync
        def _(s):
            for sem in [s_ld, s_wld, s_x0, s_xch, s_wA, s_wB, s_xld, s_ob,
                        s_tA, s_tB, s_sel, s_stg]:
                if sem.v:
                    s.wait_ge(sem.h, sem.v)
            if DEBUG_DUMP:
                dsrc = {"dr_x2": dr_x2, "dr_ob": dr_ob, "dr_dbg": dr_dbg}[DEBUG_DUMP]
                s_st.inc(s.dma_start(
                    out=out_tok.rearrange("(a p) d -> p a d", a=16),
                    in_=dsrc.rearrange("p (a d) -> p a d", a=16)), 16)
            s_st.inc(s.dma_start(out=out_id[:, :], in_=sid[:, :]), 16)
            s.wait_ge(s_st.h, s_st.v)
        for sc_ in list(reversed(open_scopes)):
            sc_.close()
        es.close()
        return nc

    flush_sync()

    @block.vector
    def _(v):
        v.memset(ctx3[:, :], 0.0)
        v.memset(sel[:, :], 0.0)
        s_ev.inc(v.memset(xcat[0][:, :], 0.0))
    MEMSET2_EV = s_ev.v

    @block.sync
    def _(s, val=MEMSET2_EV):
        s.wait_ge(s_ev.h, val)
        for k in range(32):
            s_sel.inc(s.dma_start(out=sel[k:k + 1, k * 128:(k + 1) * 128],
                                  in_=onesr_b[:, :]), 16)
    SEL_LD = s_sel.v

    barrier()
    if PHASE_LIMIT <= 1:
        return debug_finish()

    def emit_ln(src_fm, ready_sem, ready_val, bra, brb, extra_sq=(),
                extra_wr=()):
        @block.scalar
        def _(sc, sem=ready_sem, val=ready_val, ex=tuple(extra_sq)):
            sc.wait_ge(sem.h, val)
            for xs_, xv_ in ex:
                if xv_:
                    sc.wait_ge(xs_.h, xv_)
            s_sc.inc(sc.activation(out=scrA[:, :], in_=src_fm[:, :],
                                   func=AF.Square))
        sq_done = s_sc.v

        @block.tensor
        def _(t, val=sq_done, bra=bra, brb=brb):
            t.wait_ge(s_sc.h, val)
            bwait(t, bra)
            bwait(t, brb)
            for sig in range(8):
                t.matmul(bra[:, :], onesc_f[:, :], src_fm[:, bass.ts(sig, TC)],
                         start=(sig == 0), stop=(sig == 7))
            for sig in range(8):
                mm = t.matmul(brb[:, :], onesc_f[:, :],
                              scrA[:, bass.ts(sig, TC)],
                              start=(sig == 0), stop=(sig == 7))
            s_mm.inc(mm)
        red_done = s_mm.v

        base = s_ev.v

        @block.vector
        def _(v, val=red_done, base=base, bra=bra, brb=brb):
            v.wait_ge(s_mm.h, val)
            v.tensor_scalar_mul(r_mean[:, :], bra[:, :], 1.0 / D)
            s_ev.inc(v.tensor_scalar_mul(r_var[:, :], brb[:, :], 1.0 / D))
            v.wait_ge(s_ev.h, base + 1)
            s_ev.inc(v.tensor_tensor(out=r_mr[:, :], in0=r_mean[:, :],
                                     in1=r_mean[:, :], op=ALU.mult))
            v.wait_ge(s_ev.h, base + 2)
            s_ev.inc(v.tensor_tensor(out=r_var[:, :], in0=r_var[:, :],
                                     in1=r_mr[:, :], op=ALU.subtract))
        var_done = s_ev.v
        bdone(bra, s_ev, var_done)
        bdone(brb, s_ev, var_done)

        base2 = s_sc.v

        @block.scalar
        def _(sc, val=var_done, base2=base2):
            sc.wait_ge(s_ev.h, val)
            s_sc.inc(sc.activation(out=r_rstd[:, :], in_=r_var[:, :],
                                   func=AF.Ln, bias=EPS))
            sc.wait_ge(s_sc.h, base2 + 1)
            s_sc.inc(sc.activation(out=r_rstd[:, :], in_=r_rstd[:, :],
                                   func=AF.Exp, scale=-0.5))
        rstd_done = s_sc.v

        @block.vector
        def _(v, val=rstd_done):
            v.wait_ge(s_sc.h, val)
            s_ev.inc(v.tensor_tensor(out=r_mr[:, :], in0=r_mean[:, :],
                                     in1=r_rstd[:, :], op=ALU.mult))
        mr_done = s_ev.v

        @block.tensor
        def _(t, val=mr_done):
            t.wait_ge(s_ev.h, val)
            bwait(t, pE)
            bwait(t, pF)
            t.matmul(pE[:, :], onesr_f[:, :], r_rstd[:, :], start=True,
                     stop=True)
            s_mm.inc(t.matmul(pF[:, :], onesr_f[:, :], r_mr[:, :],
                              start=True, stop=True))
        bc_done = s_mm.v

        base3 = s_ev.v

        @block.vector
        def _(v, val=bc_done, base3=base3, ex=tuple(extra_wr)):
            v.wait_ge(s_mm.h, val)
            for xs_, xv_ in ex:
                if xv_:
                    v.wait_ge(xs_.h, xv_)
            v.tensor_copy(out=sbc[:, 0:TC], in_=pE[:, :])
            s_ev.inc(v.tensor_copy(out=sbc[:, TC:2 * TC], in_=pF[:, :]))
            v.wait_ge(s_ev.h, base3 + 1)
            for sig in range(8):
                mm = v.tensor_tensor(out=scrA[:, bass.ts(sig, TC)],
                                     in0=src_fm[:, bass.ts(sig, TC)],
                                     in1=sbc[:, 0:TC], op=ALU.mult)
            s_ev.inc(mm)
            v.wait_ge(s_ev.h, base3 + 2)
            for sig in range(8):
                mm = v.tensor_tensor(out=sln[:, bass.ts(sig, TC)],
                                     in0=scrA[:, bass.ts(sig, TC)],
                                     in1=sbc[:, TC:2 * TC], op=ALU.subtract)
            s_ev.inc(mm)
        bdone(pE, s_ev, s_ev.v)
        bdone(pF, s_ev, s_ev.v)
        return s_ev.v

    def emit_poll(s, flag_dyn_ap):
        rflag, rcond, riter = regs["rflag"], regs["rcond"], regs["riter"]
        s.reg_mov(rcond, 1)
        s.reg_mov(riter, 0)
        with s.While(rcond):
            s.dma_start(out=sfpoll[:, :], in_=flag_dyn_ap).then_inc(s_pl.h, 16)
            s.wait_ge(s_pl.h, 16)
            s.sem_clear(s_pl.h)
            s.reg_load(rflag, sfpoll[0:1, 0:1])
            s.reg_alu(riter, riter, 1, ALU.add)
            s.reg_alu(rcond, rflag, 1, ALU.is_lt)
            s.reg_alu(rflag, riter, POLL_CAP, ALU.is_lt)
            s.reg_alu(rcond, rcond, rflag, ALU.logical_and)

    x0tm_free = 0
    scan_prev = 0
    xcat_ok = [0, 0]
    scrA_guard = 0
    sln_free_A = 0
    prev_gate_ev = 0
    prev_bc_ld = 0
    prev_bc_mm = 0
    prev_dt_mm = 0

    for c in range(NCH):
        cxc = xcat[0]
        cxn = xcat[0]

        for tt in range(4):
            def _x0ld(s, bsel, par, c=c, tt=tt, fv=x0tm_free):
                if fv:
                    s.wait_ge(s_mm.h, fv)
                s.dma_start(
                    out=x0tm[:, :],
                    in_=x0_in.rearrange("b (ch tt p) d -> b ch tt p d", tt=4,
                                        p=128)[bsel, c, tt, :, :]
                ).then_inc(s_x0.h, 16)
            dsync(_x0ld)
            s_x0.v += 16
            tile_ld = s_x0.v

            @block.tensor
            def _(t, c=c, tt=tt, val=tile_ld):
                t.wait_ge(s_x0.h, val)
                for half in range(2):
                    pt = (pA, pB)[half]
                    bwait(t, pt)
                    for j in range(4):
                        sig = half * 4 + j
                        mm = t.transpose(
                            pt.ap().rearrange("p (j q) -> p j q", j=4)[
                                :, j, :],
                            x0tm[:, bass.ts(sig, 128)],
                            s_eye[:, :])
                    s_mm.inc(mm)
            tr_vals = [s_mm.v - 2 + i + 1 for i in range(2)]
            x0tm_free = tr_vals[-1]

            @block.vector
            def _(v, tt=tt, vals=tr_vals):
                for half in range(2):
                    pt = (pA, pB)[half]
                    v.wait_ge(s_mm.h, vals[half])
                    s_ev.inc(v.tensor_copy(
                        out=x0fm.ap().rearrange("p (sig q) -> p sig q",
                                                sig=8)[
                            :, half * 4:(half + 1) * 4,
                            tt * 128:(tt + 1) * 128],
                        in_=pt.ap().rearrange("p (j q) -> p j q", j=4)))
            bdone(pA, s_ev, s_ev.v - 1)
            bdone(pB, s_ev, s_ev.v)
        x0fm_ev = s_ev.v

        ln_ev = emit_ln(x0fm, s_ev, x0fm_ev, pRa_A, pRb_A,
                        extra_sq=[(s_xch, scrA_guard)],
                        extra_wr=[(s_xch, scrA_guard), (s_mm, sln_free_A)])

        xpre_base = s_ev.v

        @block.tensor
        def _(t, c=c, val=ln_ev, xok=xcat_ok[c % 2], xb=xpre_base):
            t.wait_ge(s_ev.h, val)
            if xok:
                t.wait_ge(s_sc.h, xok)
            for mu in range(8):
                pt = (pC, pD)[mu % 2]
                bwait(t, pt)
                if mu >= 2:
                    t.wait_ge(s_ev.h, xb + mu - 1)
                for sig in range(8):
                    mm = t.matmul(
                        pt[:, :],
                        sWinX.ap().rearrange("p (sig m q) -> p sig m q",
                                             sig=8, m=8)[:, sig, mu, :],
                        sln[:, bass.ts(sig, TC)],
                        start=(sig == 0), stop=(sig == 7))
                s_mm.inc(mm)
        winx_mm = [s_mm.v - 8 + i + 1 for i in range(8)]

        @block.vector
        def _(v, vals=winx_mm, cxc=cxc):
            for mu in range(8):
                pt = (pC, pD)[mu % 2]
                v.wait_ge(s_mm.h, vals[mu])
                s_ev.inc(v.tensor_copy(
                    out=cxc.ap().rearrange("p (mu w) -> p mu w", mu=8)[
                        :, mu, 3:515],
                    in_=pt[:, :]))
        xpre_evs = [s_ev.v - 8 + i + 1 for i in range(8)]
        bdone(pC, s_ev, xpre_evs[-2])
        bdone(pD, s_ev, xpre_evs[-1])

        zsil_base = s_sc.v

        @block.tensor
        def _(t, c=c, zb=zsil_base):
            for mu in range(8):
                pt = (pC, pD)[mu % 2]
                bwait(t, pt)
                if mu >= 2:
                    t.wait_ge(s_sc.h, zb + mu - 1)
                for sig in range(8):
                    mm = t.matmul(
                        pt[:, :],
                        sWinZ.ap().rearrange("p (sig m q) -> p sig m q",
                                             sig=8, m=8)[:, sig, mu, :],
                        sln[:, bass.ts(sig, TC)],
                        start=(sig == 0), stop=(sig == 7))
                s_mm.inc(mm)
        winz_mm = [s_mm.v - 8 + i + 1 for i in range(8)]

        @block.scalar
        def _(sc, vals=winz_mm, pg=prev_gate_ev):
            if pg:
                sc.wait_ge(s_ev.h, pg)
            for mu in range(8):
                pt = (pC, pD)[mu % 2]
                sc.wait_ge(s_mm.h, vals[mu])
                s_sc.inc(sc.activation(out=szs[:, bass.ts(mu, TC)],
                                       in_=pt[:, :], func=AF.Silu))
        z_scs = [s_sc.v - 8 + i + 1 for i in range(8)]
        bdone(pC, s_sc, z_scs[-2])
        bdone(pD, s_sc, z_scs[-1])

        conv_base_sc = s_sc.v

        conv_evs = []

        @block.vector
        def _(v, c=c, vals=xpre_evs, cxc=cxc, bsc=conv_base_sc):
            def F(instr):
                v.wait_ge(s_ev.h, s_ev.inc(instr))
            if c > 0:
                F(v.tensor_copy(
                    out=cxc.ap().rearrange("p (mu w) -> p mu w", mu=8)[
                        :, :, 0:3],
                    in_=ctx3.ap().rearrange("p (mu k) -> p mu k", mu=8)))
            for mu in range(8):
                v.wait_ge(s_ev.h, vals[mu])
                if mu >= 2:
                    v.wait_ge(s_sc.h, bsc + (mu - 2) + 1)
                ca, cb = (su_t[0], su_t[1]) if mu % 2 == 0 else (shh[0], shh[1])
                base = mu * 515
                F(v.tensor_scalar(out=ca[:, :], in0=cxc[:, base:base + TC],
                                  scalar1=sconvw[:, 4 * mu:4 * mu + 1],
                                  scalar2=None, op0=ALU.mult))
                F(v.scalar_tensor_tensor(
                    out=cb[:, :], in0=cxc[:, base + 1:base + 1 + TC],
                    scalar=sconvw[:, 4 * mu + 1:4 * mu + 2], in1=ca[:, :],
                    op0=ALU.mult, op1=ALU.add))
                F(v.scalar_tensor_tensor(
                    out=ca[:, :], in0=cxc[:, base + 2:base + 2 + TC],
                    scalar=sconvw[:, 4 * mu + 2:4 * mu + 3], in1=cb[:, :],
                    op0=ALU.mult, op1=ALU.add))
                F(v.scalar_tensor_tensor(
                    out=cb[:, :], in0=cxc[:, base + 3:base + 3 + TC],
                    scalar=sconvw[:, 4 * mu + 3:4 * mu + 4], in1=ca[:, :],
                    op0=ALU.mult, op1=ALU.add))
                s_ev.inc(v.tensor_copy(
                    out=ctx3[:, mu * 3:mu * 3 + 3],
                    in_=cxc[:, base + TC:base + TC + 3]))
                conv_evs.append(s_ev.v)

        @block.scalar
        def _(sc, vals=conv_evs, pg=prev_gate_ev):
            if pg:
                sc.wait_ge(s_ev.h, pg)
            for mu in range(8):
                buf = (su_t[1], shh[1])[mu % 2]
                sc.wait_ge(s_ev.h, vals[mu])
                s_sc.inc(sc.activation(out=sxs[:, bass.ts(mu, TC)],
                                       in_=buf[:, :], func=AF.Silu,
                                       bias=sconvb[:, mu:mu + 1]))
        xs_sc = s_sc.v
        xcat_ok[(c + 1) % 2] = xs_sc
        xcat_ok[c % 2] = xs_sc

        @block.tensor
        def _(t, val=xs_sc):
            t.wait_ge(s_sc.h, val)
            bwait(t, pE)
            for mu in range(8):
                mm = t.matmul(pE[0:96, :],
                              sWxT.ap().rearrange("p (mu j) -> p mu j",
                                                  mu=8)[:, mu, :],
                              sxs[:, bass.ts(mu, TC)],
                              start=(mu == 0), stop=(mu == 7))
            s_mm.inc(mm)
        wx_mm = s_mm.v

        @block.vector
        def _(v, val=wx_mm, pb=prev_bc_ld):
            v.wait_ge(s_mm.h, val)
            if pb:
                v.wait_ge(s_wld.h, pb)
            s_ev.inc(v.tensor_copy(out=sproj[:, :], in_=pE[0:96, :]))
        proj_ev = s_ev.v
        bdone(pE, s_ev, proj_ev)

        _pv1 = s_xch.v + 16

        def _projx(s, bsel, par, c=c, val=proj_ev, v1=_pv1):
            s.wait_ge(s_ev.h, val)
            s.dma_start(out=sh_proj[par, c, :, :],
                        in_=sproj[:, :]).then_inc(s_xch.h, 16)
            s.wait_ge(s_xch.h, v1)
            s.dma_start(out=sh_flag[par, c:c + 1, 0:1],
                        in_=sflag1[:, :]).then_inc(s_xch.h, 16)
            emit_poll(s, sh_flag[1 - par, c:c + 1, 0:1])
            s.dma_start(out=shh[0][0:96, :],
                        in_=sh_proj[1 - par, c, :, :]).then_inc(s_xch.h, 16)
        dsync(_projx)
        s_xch.v += 48
        projx_ld = s_xch.v

        @block.vector
        def _(v, val=projx_ld):
            v.wait_ge(s_xch.h, val)
            s_ev.inc(v.tensor_tensor(out=sproj[:, :], in0=sproj[:, :],
                                     in1=shh[0][0:96, :], op=ALU.add))
        projf_ev = s_ev.v

        @block.gpsimd
        def _(g, val=projf_ev, pb=prev_bc_mm):
            g.wait_ge(s_ev.h, val)
            if pb:
                g.wait_ge(s_mm.h, pb)
            s_wld.inc(g.dma_start(out=sBC32[:, :], in_=sproj[64:96, :]), 16)
        bc_ld = s_wld.v

        @block.vector
        def _(v, val=projf_ev, pd=prev_dt_mm):
            v.wait_ge(s_ev.h, val)
            if pd:
                v.wait_ge(s_mm.h, pd)
            s_ev.inc(v.tensor_copy(out=sBb[0:64, :], in_=sproj[0:64, :]))
        dtl_ev = s_ev.v

        dtsp_base = s_sc.v

        @block.tensor
        def _(t, val=dtl_ev, db=dtsp_base):
            t.wait_ge(s_ev.h, val)
            for mu in range(8):
                pt = (pC, pD)[mu % 2]
                bwait(t, pt)
                if mu >= 2:
                    t.wait_ge(s_sc.h, db + 3 * (mu - 2) + 1)
                s_mm.inc(t.matmul(
                    pt[:, :],
                    sWdt.ap().rearrange("p (mu q) -> p mu q", mu=8)[:, mu, :],
                    sBb[0:64, :], start=True, stop=True))
        dt_mms = [s_mm.v - 8 + i + 1 for i in range(8)]

        sp_base_sc = s_sc.v
        assert sp_base_sc == dtsp_base
        sp_base_ev = s_ev.v

        @block.scalar
        def _(sc, vals=dt_mms, bsc=sp_base_sc, bev=sp_base_ev,
              pg=prev_gate_ev):
            if pg:
                sc.wait_ge(s_ev.h, pg)
            for mu in range(8):
                pt = (pC, pD)[mu % 2]
                ta, tb = (sa_t[0], sa_t[1]) if mu % 2 == 0 else \
                    (su_t[0], su_t[1])
                sc.wait_ge(s_mm.h, vals[mu])
                if mu >= 2:
                    sc.wait_ge(s_ev.h, bev + (mu - 2) + 1)
                sc.activation(out=ta[:, :], in_=pt[:, :], func=AF.Abs,
                              bias=sbdt[:, mu:mu + 1])
                s_sc.inc(sc.activation(out=tb[:, :], in_=pt[:, :],
                                       func=AF.Relu, bias=sbdt[:, mu:mu + 1]))
                sc.wait_ge(s_sc.h, bsc + 3 * mu + 1)
                s_sc.inc(sc.activation(out=ta[:, :], in_=ta[:, :],
                                       func=AF.Exp, scale=-1.0))
                sc.wait_ge(s_sc.h, bsc + 3 * mu + 2)
                s_sc.inc(sc.activation(out=ta[:, :], in_=ta[:, :],
                                       func=AF.Ln, bias=1.0))
        dt_sps = [sp_base_sc + 3 * i + 3 for i in range(8)]
        assert s_sc.v == sp_base_sc + 24

        @block.vector
        def _(v, vals=dt_sps, bev=sp_base_ev):
            for mu in range(8):
                ta, tb = (sa_t[0], sa_t[1]) if mu % 2 == 0 else \
                    (su_t[0], su_t[1])
                v.wait_ge(s_sc.h, vals[mu])
                s_ev.inc(v.tensor_tensor(out=sdt[:, bass.ts(mu, TC)],
                                         in0=ta[:, :], in1=tb[:, :],
                                         op=ALU.add))
            v.wait_ge(s_ev.h, s_ev.v)
            s_ev.inc(v.tensor_tensor(out=sdtx[:, :], in0=sdt[:, :],
                                     in1=sxs[:, :], op=ALU.mult))
        dt_evs = [sp_base_ev + i + 1 for i in range(8)]
        dtx_ev = s_ev.v
        bdone(pC, s_ev, dt_evs[-2])
        bdone(pD, s_ev, dt_evs[-1])

        scan_base_ev = s_ev.v
        bc_evac_vals = []
        scan_op_vals = {}
        grp_vals = {}
        _v = scan_base_ev
        for si in range(NS):
            _v += 1
            bc_evac_vals.append(_v)
            for mu in range(8):
                _v += 1
                _v += 1
                scan_op_vals[(si, mu)] = _v
                if si > 0:
                    _v += 1
                _v += 1
                grp_vals[(si, mu)] = _v
        scan_end_ev = _v

        @block.tensor
        def _(t, val=bc_ld, bev=bc_evac_vals, selv=SEL_LD):
            t.wait_ge(s_wld.h, val)
            t.wait_ge(s_sel.h, selv)
            for si in range(NS):
                if si >= 1:
                    t.wait_ge(s_ev.h, bev[si - 1])
                bwait(t, pE)
                bwait(t, pF)
                t.matmul(pE[:, :], sel[:, bass.ts(si, 128)],
                         sBC32[:, :], start=True, stop=True)
                s_mm.inc(t.matmul(
                    pF[:, :], sel[:, bass.ts(NS + si, 128)],
                    sBC32[:, :], start=True, stop=True))
        bc_mms = [s_mm.v - NS + i + 1 for i in range(NS)]
        bdone(pE, s_ev, bc_evac_vals[-1])
        bdone(pF, s_ev, bc_evac_vals[-1])

        @block.scalar
        def _(sc, dtv=dt_evs, sov=dict(scan_op_vals)):
            for si in range(NS):
                for mu in range(8):
                    k = si * 8 + mu
                    buf = sa_t[k % 2]
                    if si == 0:
                        sc.wait_ge(s_ev.h, dtv[mu])
                    if k >= 2:
                        km2 = k - 2
                        sc.wait_ge(s_ev.h, sov[(km2 // 8, km2 % 8)])
                    s_sc.inc(sc.activation(
                        out=buf[:, :], in_=sdt[:, bass.ts(mu, TC)],
                        func=AF.Exp,
                        scale=sA[:, mu * 16 + si:mu * 16 + si + 1]))
        a_scs = [s_sc.v - NS * 8 + i + 1 for i in range(NS * 8)]

        @block.vector
        def _(v, bcv=bc_mms, asv=a_scs, dxv=dtx_ev, dtm=dt_mms[-1]):
            def F(instr):
                v.wait_ge(s_ev.h, s_ev.inc(instr))
            v.wait_ge(s_ev.h, dxv)
            v.wait_ge(s_mm.h, dtm)
            for si in range(NS):
                v.wait_ge(s_mm.h, bcv[si])
                v.tensor_copy(out=sBb[:, :], in_=pE[:, :])
                F(v.tensor_copy(out=sCb[:, :], in_=pF[:, :]))
                for mu in range(8):
                    k = si * 8 + mu
                    sabuf = sa_t[k % 2]
                    hbuf = shh[k % 2]
                    col = mu * 16 + si
                    v.wait_ge(s_sc.h, asv[k])
                    F(v.tensor_tensor(out=su_t[k % 2][:, :],
                                      in0=sdtx[:, bass.ts(mu, TC)],
                                      in1=sBb[:, :], op=ALU.mult))
                    F(v.tensor_tensor_scan(out=hbuf[:, :], data0=sabuf[:, :],
                                           data1=su_t[k % 2][:, :],
                                           initial=hstate[:, col:col + 1],
                                           op0=ALU.mult, op1=ALU.add))
                    v.tensor_copy(out=hstate[:, col:col + 1],
                                  in_=hbuf[:, TC - 1:TC])
                    if si == 0:
                        s_ev.inc(v.tensor_tensor(
                            out=scrA[:, bass.ts(mu, TC)], in0=hbuf[:, :],
                            in1=sCb[:, :], op=ALU.mult))
                    else:
                        F(v.tensor_tensor(out=syp[:, :], in0=hbuf[:, :],
                                          in1=sCb[:, :], op=ALU.mult))
                        s_ev.inc(v.tensor_tensor(
                            out=scrA[:, bass.ts(mu, TC)],
                            in0=scrA[:, bass.ts(mu, TC)], in1=syp[:, :],
                            op=ALU.add))
        assert s_ev.v == scan_end_ev, (s_ev.v, scan_end_ev)
        scan_ev = s_ev.v
        scan_prev = scan_ev

        @block.vector
        def _(v, zv=z_scs[-1], sev=scan_ev):
            def F(instr):
                v.wait_ge(s_ev.h, s_ev.inc(instr))
            v.wait_ge(s_sc.h, zv)
            v.wait_ge(s_ev.h, sev)
            for mu in range(8):
                F(v.scalar_tensor_tensor(out=syp[:, :],
                                         in0=sxs[:, bass.ts(mu, TC)],
                                         scalar=sDv[:, mu:mu + 1],
                                         in1=scrA[:, bass.ts(mu, TC)],
                                         op0=ALU.mult, op1=ALU.add))
                F(v.tensor_tensor(out=sdtx[:, bass.ts(mu, TC)],
                                  in0=syp[:, :],
                                  in1=szs[:, bass.ts(mu, TC)],
                                  op=ALU.mult))
        gate_ev = s_ev.v

        wo_base = s_ev.v

        @block.tensor
        def _(t, val=gate_ev, wb=wo_base):
            t.wait_ge(s_ev.h, val)
            for m in range(8):
                pt = (pC, pD)[m % 2]
                bwait(t, pt)
                if m >= 2:
                    t.wait_ge(s_ev.h, wb + m - 1)
                for mu in range(8):
                    mm = t.matmul(
                        pt[:, :],
                        sWout.ap().rearrange("p (mu m q) -> p mu m q",
                                             mu=8, m=8)[:, mu, m, :],
                        sdtx[:, bass.ts(mu, TC)],
                        start=(mu == 0), stop=(mu == 7))
                s_mm.inc(mm)
        wo_mms = [s_mm.v - 8 + i + 1 for i in range(8)]

        @block.vector
        def _(v, vals=wo_mms):
            for m in range(8):
                pt = (pC, pD)[m % 2]
                v.wait_ge(s_mm.h, vals[m])
                s_ev.inc(v.scalar_tensor_tensor(
                    out=scrA[:, bass.ts(m, TC)], in0=x0fm[:, bass.ts(m, TC)],
                    scalar=0.5, in1=pt[:, :], op0=ALU.mult, op1=ALU.add))
        po_evs = [s_ev.v - 8 + i + 1 for i in range(8)]
        pout_ev = s_ev.v
        bdone(pC, s_ev, po_evs[-2])
        bdone(pD, s_ev, po_evs[-1])

        _pv2 = s_xch.v + 16

        def _poutw(s, bsel, par, c=c, val=pout_ev, v1=_pv2):
            s.wait_ge(s_ev.h, val)
            s.dma_start(
                out=sh_pout.rearrange("a c (m p) q -> a c p m q", m=8)[
                    par, c, :, :, :],
                in_=scrA.ap().rearrange("p (m q) -> p m q", m=8)
            ).then_inc(s_xch.h, 16)
            s.wait_ge(s_xch.h, v1)
            s.dma_start(out=sh_flag[par, 8 + c:9 + c, 0:1],
                        in_=sflag1[:, :]).then_inc(s_xch.h, 16)
        dsync(_poutw)
        s_xch.v += 32
        scrA_guard = s_xch.v
        sln_free_A = winz_mm[-1]
        prev_gate_ev = gate_ev
        prev_bc_ld = bc_ld
        prev_bc_mm = bc_mms[-1]
        prev_dt_mm = dt_mms[-1]

    flush_sync()
    barrier()
    if PHASE_LIMIT <= 2:
        return debug_finish()
    scA.close()

    scHA = Scope()
    hA = scHA.SB("hA", [128, 16384], BF16)
    scX = Scope()
    X = scX.SB("X", [128, 16384], F32)
    hB = scX.SB("hB", [128, 16384], BF16)
    scB1 = Scope()
    wbuf = [scB1.SB(f"wbuf{i}", [128, 4096], BF16) for i in range(2)]
    spo = scB1.SB("spo", [128, 4096], F32)
    fst1 = scB1.SB("fst1", [128, 512], F32)
    scB1b = Scope()
    pR1 = scB1b.PSUM("pR1", [1, 512])
    pR2 = scB1b.PSUM("pR2", [1, 512])

    wbuf_free = [0, 0]

    def emit_ffn(xoff, ln_ready, wbuf, fst1, hA, hB, X):
        nonlocal wbuf_free
        last_w2_mm = 0
        g1_scs = []
        for m in range(32):
            wb = wbuf[m % 2]
            ws = (s_wA, s_wB)[m % 2]

            @block.gpsimd
            def _(g, m=m, wb=wb, ws=ws, pu=wbuf_free[m % 2]):
                if pu:
                    g.wait_ge(s_mm.h, pu)
                ws.inc(g.dma_start(
                    out=wb.ap()[:, 0:1024].rearrange("p (sig q) -> p sig q",
                                                     sig=8),
                    in_=w1T.rearrange("(sig p) (m q) -> p sig m q", sig=8,
                                      m=32)[:, :, m, :]), 16)
            wld = ws.v

            @block.tensor
            def _(t, m=m, wb=wb, ws=ws, val=wld, lnv=ln_ready):
                t.wait_ge(ws.h, val)
                if m == 0:
                    t.wait_ge(s_ev.h, lnv)
                pt = (pC, pD)[m % 2]
                bwait(t, pt)
                for sig in range(8):
                    mm = t.matmul(
                        pt[:, :],
                        wb.ap()[:, 0:1024].rearrange("p (sig q) -> p sig q",
                                                     sig=8)[:, sig, :],
                        sln[:, bass.ts(sig, TC)],
                        start=(sig == 0), stop=(sig == 7))
                s_mm.inc(mm)
            mmv = s_mm.v
            wbuf_free[m % 2] = mmv

            @block.scalar
            def _(sc, m=m, val=mmv):
                pt = (pC, pD)[m % 2]
                sc.wait_ge(s_mm.h, val)
                s_sc.inc(sc.activation(out=hB[:, bass.ts(m, TC)],
                                       in_=pt[:, :], func=AF.Silu,
                                       scale=sw1s[:, m:m + 1]))
            g1_scs.append(s_sc.v)
            bdone((pC, pD)[m % 2], s_sc, s_sc.v)
        g1_done = s_sc.v

        for m in range(32):
            wb = wbuf[m % 2]
            ws = (s_wA, s_wB)[m % 2]

            @block.gpsimd
            def _(g, m=m, wb=wb, ws=ws, pu=wbuf_free[m % 2]):
                if pu:
                    g.wait_ge(s_mm.h, pu)
                ws.inc(g.dma_start(
                    out=wb.ap()[:, 0:1024].rearrange("p (sig q) -> p sig q",
                                                     sig=8),
                    in_=w2T.rearrange("(sig p) (m q) -> p sig m q", sig=8,
                                      m=32)[:, :, m, :]), 16)
            wld = ws.v

            @block.tensor
            def _(t, m=m, wb=wb, ws=ws, val=wld):
                t.wait_ge(ws.h, val)
                pt = (pC, pD)[m % 2]
                bwait(t, pt)
                for sig in range(8):
                    mm = t.matmul(
                        pt[:, :],
                        wb.ap()[:, 0:1024].rearrange("p (sig q) -> p sig q",
                                                     sig=8)[:, sig, :],
                        sln[:, bass.ts(sig, TC)],
                        start=(sig == 0), stop=(sig == 7))
                s_mm.inc(mm)
            mmv = s_mm.v
            wbuf_free[m % 2] = mmv
            last_w2_mm = mmv

            @block.vector
            def _(v, m=m, val=mmv, gv=g1_scs[m]):
                pt = (pC, pD)[m % 2]
                v.wait_ge(s_mm.h, val)
                v.wait_ge(s_sc.h, gv)
                s_ev.inc(v.scalar_tensor_tensor(
                    out=hA[:, bass.ts(m, TC)], in0=pt[:, :],
                    scalar=sw2s[:, m:m + 1], in1=hB[:, bass.ts(m, TC)],
                    op0=ALU.mult, op1=ALU.mult))
            bdone((pC, pD)[m % 2], s_ev, s_ev.v)
        g_done = s_ev.v

        for stage, (hT, scl_t, src, dst) in enumerate(
                [(h1T, sh1s, hA, hB), (h2T, sh2s, hB, hA)]):
            src_sem = s_ev if stage == 0 else s_sc
            src_val = g_done if stage == 0 else s_sc.v
            for m in range(32):
                wb = wbuf[m % 2]
                ws = (s_wA, s_wB)[m % 2]

                @block.gpsimd
                def _(g, m=m, hT=hT, wb=wb, ws=ws, pu=wbuf_free[m % 2]):
                    if pu:
                        g.wait_ge(s_mm.h, pu)
                    ws.inc(g.dma_start(
                        out=wb.ap().rearrange("p (sig q) -> p sig q", sig=32),
                        in_=hT.rearrange("(sig p) (m q) -> p sig m q",
                                         sig=32, m=32)[:, :, m, :]), 16)
                wld = ws.v

                @block.tensor
                def _(t, m=m, wb=wb, ws=ws, val=wld, src=src, ssem=src_sem,
                      sval=src_val):
                    t.wait_ge(ws.h, val)
                    if m == 0:
                        t.wait_ge(ssem.h, sval)
                    pt = (pC, pD)[m % 2]
                    bwait(t, pt)
                    for sig in range(32):
                        mm = t.matmul(
                            pt[:, :],
                            wb.ap().rearrange("p (sig q) -> p sig q",
                                              sig=32)[:, sig, :],
                            src[:, sig * 128:(sig + 1) * 128] if False else
                            src[:, bass.ts(sig, TC)][:, 0:TC],
                            start=(sig == 0), stop=(sig == 31))
                    s_mm.inc(mm)
                mmv = s_mm.v
                wbuf_free[m % 2] = mmv

                @block.scalar
                def _(sc, m=m, val=mmv, scl_t=scl_t, dst=dst):
                    pt = (pC, pD)[m % 2]
                    sc.wait_ge(s_mm.h, val)
                    s_sc.inc(sc.activation(out=dst[:, bass.ts(m, TC)],
                                           in_=pt[:, :], func=AF.Gelu,
                                           scale=scl_t[:, m:m + 1]))
                bdone((pC, pD)[m % 2], s_sc, s_sc.v)
        h_done = s_sc.v

        res_evs = []
        for m in range(8):
            wb = wbuf[m % 2]
            ws = (s_wA, s_wB)[m % 2]

            @block.gpsimd
            def _(g, m=m, wb=wb, ws=ws, pu=wbuf_free[m % 2]):
                if pu:
                    g.wait_ge(s_mm.h, pu)
                ws.inc(g.dma_start(
                    out=wb.ap().rearrange("p (sig q) -> p sig q", sig=32),
                    in_=w3T.rearrange("(sig p) (m q) -> p sig m q", sig=32,
                                      m=8)[:, :, m, :]), 16)
            wld = ws.v

            @block.tensor
            def _(t, m=m, wb=wb, ws=ws, val=wld, hd=h_done):
                t.wait_ge(ws.h, val)
                if m == 0:
                    t.wait_ge(s_sc.h, hd)
                pt = (pC, pD)[m % 2]
                bwait(t, pt)
                for sig in range(32):
                    mm = t.matmul(
                        pt[:, :],
                        wb.ap().rearrange("p (sig q) -> p sig q", sig=32)[
                            :, sig, :],
                        hA[:, bass.ts(sig, TC)][:, 0:TC],
                        start=(sig == 0), stop=(sig == 31))
                s_mm.inc(mm)
            mmv = s_mm.v
            wbuf_free[m % 2] = mmv

            base = s_ev.v

            @block.vector
            def _(v, m=m, val=mmv, xoff=xoff, base=base):
                pt = (pC, pD)[m % 2]
                v.wait_ge(s_mm.h, val)
                s_ev.inc(v.tensor_scalar(out=fst1[:, :], in0=pt[:, :],
                                         scalar1=sw3s[:, m:m + 1],
                                         scalar2=None, op0=ALU.mult))
                v.wait_ge(s_ev.h, base + 1)
                s_ev.inc(v.tensor_tensor(
                    out=X[:, xoff + m * TC:xoff + (m + 1) * TC],
                    in0=X[:, xoff + m * TC:xoff + (m + 1) * TC],
                    in1=fst1[:, :], op=ALU.add))
            res_evs.append(s_ev.v)
            bdone((pC, pD)[m % 2], s_ev, s_ev.v)
        return s_ev.v, last_w2_mm

    sln_free_B = 0
    spo_free_B = 0
    for sl in range(4):
        def _x1ld(s, bsel, par, sl=sl, spf=spo_free_B):
            if spf:
                s.wait_ge(s_ev.h, spf)
            g = 4 * par + sl
            emit_poll(s, sh_flag[1 - par, 8 + g:9 + g, 0:1])
            s.dma_start(
                out=X.ap()[:, bass.ts(sl, 4096)].rearrange(
                    "p (m q) -> p m q", m=8),
                in_=sh_pout.rearrange("a c (m p) q -> a c p m q", m=8)[
                    par, g, :, :, :]).then_inc(s_xld.h, 16)
            s.dma_start(
                out=spo.ap().rearrange("p (m q) -> p m q", m=8),
                in_=sh_pout.rearrange("a c (m p) q -> a c p m q", m=8)[
                    1 - par, g, :, :, :]).then_inc(s_xld.h, 16)
        dsync(_x1ld)
        s_xld.v += 32
        x1_ld = s_xld.v

        @block.vector
        def _(v, sl=sl, val=x1_ld):
            v.wait_ge(s_xld.h, val)
            s_ev.inc(v.tensor_tensor(out=X[:, bass.ts(sl, 4096)],
                                     in0=X[:, bass.ts(sl, 4096)],
                                     in1=spo[:, :], op=ALU.add))
        x1_ev = s_ev.v
        spo_free_B = x1_ev

        lnv = emit_ln(X.ap()[:, bass.ts(sl, 4096)], s_ev, x1_ev, pR1, pR2,
                      extra_wr=[(s_mm, sln_free_B)])
        _, sln_free_B = emit_ffn(sl * 4096, lnv, wbuf, fst1, hA, hB, X)

    flush_sync()
    barrier()
    if PHASE_LIMIT <= 3:
        return debug_finish()
    scB1.close()

    scW = Scope()
    swT = scW.SB("swT", [128, 8192], BF16, side="right")
    scCp = Scope()
    sstg = scCp.SB("sstg", [128, 2048], BF16)

    @block.gpsimd
    def _(g):
        s_wld.inc(g.dma_start(
            out=swT.ap().rearrange("p (sig m) -> p sig m", sig=8),
            in_=W_spinT.rearrange("(sig p) m -> p sig m", sig=8)), 16)
    wspin_ld = s_wld.v

    for sl in range(4):
        lnv = emit_ln(X.ap()[:, bass.ts(sl, 4096)], s_ev, s_ev.v, pR1, pR2)

        @block.vector
        def _(v, sl=sl, val=lnv):
            v.wait_ge(s_ev.h, val)
            s_ev.inc(v.tensor_copy(out=hB[:, bass.ts(sl, 4096)],
                                   in_=sln[:, :]))
    ln2_all = s_ev.v

    @block.sync
    def _(s, val=ln2_all):
        s.wait_ge(s_ev.h, val)
        s_xch.inc(s.dma_start(out=dr_x2[:, :], in_=X[:, :]), 16)
    x2_dumped = s_xch.v

    sstg_guard = 0
    for m in range(8):
        sev_base = s_ev.v

        @block.tensor
        def _(t, m=m, val=wspin_ld, lv=ln2_all, sb=sev_base):
            t.wait_ge(s_wld.h, val)
            if m == 0:
                t.wait_ge(s_ev.h, lv)
            for nt in range(4):
                pt = (pC, pD)[nt % 2]
                bwait(t, pt)
                if nt >= 2:
                    t.wait_ge(s_ev.h, sb + nt - 1)
                for sig in range(8):
                    mm = t.matmul(
                        pt[:, :],
                        swT.ap().rearrange("p (sig m) -> p sig m", sig=8)[
                            :, sig, bass.ts(m, 128)],
                        hB.ap().rearrange("p (sl sig q) -> p sl sig q",
                                          sl=4, sig=8)[:, nt, sig, :],
                        start=(sig == 0), stop=(sig == 7))
                s_mm.inc(mm)
        s_mms = [s_mm.v - 4 + i + 1 for i in range(4)]

        @block.vector
        def _(v, m=m, vals=s_mms, gv=sstg_guard):
            if gv:
                v.wait_ge(s_xch.h, gv)
            for nt in range(4):
                pt = (pC, pD)[nt % 2]
                v.wait_ge(s_mm.h, vals[nt])
                s_ev.inc(v.tensor_scalar(out=sstg[:, bass.ts(nt, TC)],
                                         in0=pt[:, :],
                                         scalar1=sbspin[:, m:m + 1],
                                         scalar2=None, op0=ALU.add))
            v.wait_ge(s_ev.h, s_ev.v)
            s_ev.inc(v.reduce_sum(smean[:, m:m + 1], sstg[:, :], axis=AX.X))
        scv = [s_ev.v - 5 + i + 1 for i in range(4)]
        bdone(pC, s_ev, scv[-2])
        bdone(pD, s_ev, scv[-1])

        def _sstore(s, bsel, par, m=m, val=scv[-1]):
            s.wait_ge(s_ev.h, val)
            s.dma_start(out=sh_s[par, m * 128:(m + 1) * 128, :],
                        in_=sstg[:, :]).then_inc(s_xch.h, 16)
        dsync(_sstore)
        s_xch.v += 16
        st_ld = s_xch.v
        sstg_guard = st_ld
    s_written = s_xch.v
    own_mean_ev = s_ev.v

    def _sflag(s, bsel, par, val=s_written):
        s.wait_ge(s_xch.h, val)
        s.dma_start(out=sh_flag[par, 16:17, 0:1],
                    in_=sflag1[:, :]).then_inc(s_xch.h, 16)
    dsync(_sflag)
    s_xch.v += 16
    s_flagged = s_xch.v

    _mv1 = s_xch.v + 16

    def _meanx(s, bsel, par, val=own_mean_ev, v1=_mv1):
        s.wait_ge(s_ev.h, val)
        s.dma_start(out=sh_mean[par, :, :],
                    in_=smean[:, :]).then_inc(s_xch.h, 16)
        s.wait_ge(s_xch.h, v1)
        s.dma_start(out=sh_flag[par, 17:18, 0:1],
                    in_=sflag1[:, :]).then_inc(s_xch.h, 16)
        emit_poll(s, sh_flag[1 - par, 17:18, 0:1])
        s.dma_start(out=smean_p[:, :],
                    in_=sh_mean[1 - par, :, :]).then_inc(s_xch.h, 16)
    dsync(_meanx)
    s_xch.v += 48
    meanx_ld = s_xch.v

    mb_base = s_ev.v

    @block.vector
    def _(v, val=meanx_ld, base=mb_base, omv=own_mean_ev):
        v.wait_ge(s_xch.h, val)
        v.wait_ge(s_ev.h, omv)
        s_ev.inc(v.tensor_tensor(out=smean[:, :], in0=smean[:, :],
                                 in1=smean_p[:, :], op=ALU.add))
        v.wait_ge(s_ev.h, base + 1)
        s_ev.inc(v.tensor_scalar(out=smean_b[:, :], in0=smean[:, :],
                                 scalar1=1.0 / 4096.0, scalar2=None,
                                 op0=ALU.mult))
    mean_ev = s_ev.v

    @block.gpsimd
    def _(g, val=s_mm.v):
        g.wait_ge(s_mm.h, val)
        s_wld.inc(g.dma_start(
            out=swT.ap().rearrange("p (sig m) -> p sig m", sig=8),
            in_=W_JT.rearrange("(sig p) m -> p sig m", sig=8)), 16)
    wj_ld = s_wld.v

    flush_sync()
    barrier()
    if PHASE_LIMIT <= 4:
        return debug_finish()
    scCp.close()
    scX.close()
    scB1b.close()

    scC = Scope()
    sfm = scC.SB("sfm", [128, 32768], BF16)
    sTb = [scC.SB(f"sTb{i}", [128, 512], BF16) for i in range(2)]
    sprobs = scC.SB("sprobs", [128, 4096], BF16)
    sprobT = scC.SB("sprobT", [128, 16384], BF16)
    sloc = scC.SB("sloc", [128, 2048], F32)
    ep1 = scC.SB("ep1", [128, 512], F32)
    smaxv = scC.SB("smaxv", [128, 1], F32)
    snmax = scC.SB("snmax", [128, 1], F32)
    ssum = scC.SB("ssum", [128, 1], F32)
    srec = scC.SB("srec", [128, 1], F32)
    pTb = scC.PSUM("pTb", [128, 1024], BF16)

    def _sfmld(s, bsel, par, val=s_flagged):
        s.wait_ge(s_xch.h, val)
        emit_poll(s, sh_flag[1 - par, 16:17, 0:1])
        s.dma_start(
            out=sfm.ap().rearrange("p (sig m) -> p sig m", sig=8)[
                :, :, 0:2048],
            in_=sh_s.rearrange("a (sig p) m -> a p sig m", sig=8)[
                par, :, :, :]).then_inc(s_xld.h, 16)
        s.dma_start(
            out=sfm.ap().rearrange("p (sig m) -> p sig m", sig=8)[
                :, :, 2048:4096],
            in_=sh_s.rearrange("a (sig p) m -> a p sig m", sig=8)[
                1 - par, :, :, :]).then_inc(s_xld.h, 16)
    dsync(_sfmld)
    s_xld.v += 32
    sfm_ld = s_xld.v

    for m in range(8):
        eev_base = s_ev.v

        @block.tensor
        def _(t, m=m, val=wj_ld, sfv=sfm_ld, eb=eev_base):
            t.wait_ge(s_wld.h, val)
            if m == 0:
                t.wait_ge(s_xld.h, sfv)
            for nt in range(4):
                pt = (pC, pD)[nt % 2]
                bwait(t, pt)
                if nt >= 2:
                    t.wait_ge(s_ev.h, eb + nt - 1)
                for sig in range(8):
                    mm = t.matmul(
                        pt[:, :],
                        swT.ap().rearrange("p (sig m) -> p sig m", sig=8)[
                            :, sig, bass.ts(m, 128)],
                        sfm.ap().rearrange("p (sig k) -> p sig k", sig=8)[
                            :, sig, nt * TC:(nt + 1) * TC],
                        start=(sig == 0), stop=(sig == 7))
                s_mm.inc(mm)
        e_mms = [s_mm.v - 4 + i + 1 for i in range(4)]

        @block.vector
        def _(v, m=m, vals=e_mms):
            for nt in range(4):
                pt = (pC, pD)[nt % 2]
                v.wait_ge(s_mm.h, vals[nt])
                s_ev.inc(v.tensor_copy(
                    out=hA[:, m * 2048 + nt * TC:m * 2048 + (nt + 1) * TC],
                    in_=pt[:, :]))
        e_evs = [s_ev.v - 4 + i + 1 for i in range(4)]
        bdone(pC, s_ev, e_evs[-2])
        bdone(pD, s_ev, e_evs[-1])
    energy_ev = s_ev.v

    @block.gpsimd
    def _(g, val=s_mm.v):
        g.wait_ge(s_mm.h, val)
        s_wld.inc(g.dma_start(
            out=swT.ap().rearrange("p (sig m) -> p sig m", sig=8),
            in_=W_globT.rearrange("(sig p) m -> p sig m", sig=8)), 16)
    wg_ld = s_wld.v

    @block.tensor
    def _(t, val=wg_ld, mv=mean_ev):
        t.wait_ge(s_wld.h, val)
        t.wait_ge(s_ev.h, mv)
        bwait(t, pE)
        for m in range(8):
            for sig in range(8):
                mm = t.matmul(
                    pE[:, m:m + 1],
                    swT.ap().rearrange("p (sig m) -> p sig m", sig=8)[
                        :, sig, bass.ts(m, 128)],
                    smean_b[:, sig:sig + 1],
                    start=(sig == 0), stop=(sig == 7), skip_group_check=True)
        s_mm.inc(mm)
    glob_mm = s_mm.v

    @block.vector
    def _(v, val=glob_mm):
        v.wait_ge(s_mm.h, val)
        s_ev.inc(v.tensor_copy(out=sglob[:, :], in_=pE[:, 0:8]))
    glob_ev = s_ev.v
    bdone(pE, s_ev, glob_ev)
    scW.close()

    sT_free = [0, 0]
    prev_prob_sc = 0
    prev_pv_mm = 0
    for ng in range(4):
        for qt in range(4):
            nq0 = ng * 512 + qt * 128

            scb = s_ev.v

            @block.tensor
            def _(t, ng=ng, qt=qt, nq0=nq0, ev=energy_ev, sv=sfm_ld,
                  scb=scb):
                if ng == 0 and qt == 0:
                    t.wait_ge(s_ev.h, ev)
                    t.wait_ge(s_xld.h, sv)
                for mt in range(8):
                    pt = (pC, pD)[mt % 2]
                    bwait(t, pt)
                    if mt >= 2:
                        t.wait_ge(s_ev.h, scb + mt - 1)
                    for sig in range(8):
                        mm = t.matmul(
                            pt[:, :],
                            hA.ap().rearrange("p (sig n) -> p sig n",
                                              sig=8)[:, sig, nq0:nq0 + 128],
                            sfm.ap().rearrange("p (sig m) -> p sig m",
                                               sig=8)[:, sig, bass.ts(mt, TC)],
                            start=(sig == 0), stop=(sig == 7))
                    s_mm.inc(mm)
            sc_mms = [s_mm.v - 8 + i + 1 for i in range(8)]

            @block.vector
            def _(v, vals=sc_mms, pps=prev_prob_sc):
                if pps:
                    v.wait_ge(s_sc.h, pps)
                for mt in range(8):
                    pt = (pC, pD)[mt % 2]
                    v.wait_ge(s_mm.h, vals[mt])
                    s_ev.inc(v.tensor_copy(out=scrA[:, bass.ts(mt, TC)],
                                           in_=pt[:, :]))
            scev = [s_ev.v - 8 + i + 1 for i in range(8)]
            bdone(pC, s_ev, scev[-2])
            bdone(pD, s_ev, scev[-1])

            if DEBUG_ATTN and ng == 0 and qt == 0:
                def _dbg1(s, bsel, par, val=scev[-1]):
                    s.wait_ge(s_ev.h, val)
                    s.dma_start(out=dr_dbg[:, 0:4096],
                                in_=scrA[:, :]).then_inc(s_ob.h, 16)
                    s.dma_start(out=dr_dbg[:, 8192:12288],
                                in_=scrA[:, :]).then_inc(s_ob.h, 16)
                    s.dma_start(out=dr_dbg[:, 12288:16384],
                                in_=scrA[:, :]).then_inc(s_ob.h, 16)
                dsync(_dbg1)
                s_ob.v += 48
                DBG1 = s_ob.v

                @block.vector
                def _(v, val=DBG1):
                    v.wait_ge(s_ob.h, val)

            mxb = s_ev.v

            @block.vector
            def _(v, base=mxb):
                v.wait_ge(s_ev.h, base)
                s_ev.inc(v.reduce_max(smaxv[:, :], scrA[:, :], axis=AX.X))
                v.wait_ge(s_ev.h, base + 1)
                s_ev.inc(v.tensor_scalar(out=snmax[:, :], in0=smaxv[:, :],
                                         scalar1=-1.0 / 32.0, scalar2=None,
                                         op0=ALU.mult))
            nmax_ev = s_ev.v

            @block.scalar
            def _(sc, val=nmax_ev):
                sc.wait_ge(s_ev.h, val)
                s_sc.inc(sc.activation(out=sprobs[:, :], in_=scrA[:, :],
                                       func=AF.Exp, scale=1.0 / 32.0,
                                       bias=snmax[:, 0:1],
                                       accum_out=ssum[:, 0:1]))
            prob_sc = s_sc.v
            prev_prob_sc = prob_sc

            rec_base = s_ev.v

            @block.vector
            def _(v, val=prob_sc, base=rec_base):
                v.wait_ge(s_sc.h, val)
                s_ev.inc(v.reciprocal(srec[:, :], ssum[:, :]))
                v.wait_ge(s_ev.h, base + 1)
                s_ev.inc(v.tensor_scalar(out=sprobs[:, :], in0=sprobs[:, :],
                                         scalar1=srec[:, 0:1], scalar2=None,
                                         op0=ALU.mult))
            probs_ev = s_ev.v

            if DEBUG_ATTN and ng == 0 and qt == 0:
                @block.gpsimd
                def _(g, val=probs_ev):
                    g.wait_ge(s_ev.h, val)
                    s_wld.inc(g.dma_start(out=dr_dbg[:, 4096:8192],
                                          in_=sprobs[:, :]), 16)
                DBG2 = s_wld.v

                @block.tensor
                def _(t, val=DBG2):
                    t.wait_ge(s_wld.h, val)

            for grp in range(4):
                @block.tensor
                def _(t, grp=grp, val=probs_ev):
                    if grp == 0:
                        t.wait_ge(s_ev.h, val)
                    bwait(t, pTb)
                    for j in range(8):
                        mj = grp * 8 + j
                        mm = t.transpose(
                            pTb.ap().rearrange("p (j q) -> p j q", j=8)[
                                :, j, :],
                            sprobs[:, mj * 128:(mj + 1) * 128],
                            s_eye_b[:, :])
                    s_mm.inc(mm)
                tr_mm = s_mm.v

                @block.vector
                def _(v, grp=grp, qt=qt, val=tr_mm, ppv=prev_pv_mm):
                    if ppv:
                        v.wait_ge(s_mm.h, ppv)
                    v.wait_ge(s_mm.h, val)
                    s_ev.inc(v.tensor_copy(
                        out=sprobT.ap().rearrange("p (mj q) -> p mj q",
                                                  mj=32)[
                            :, grp * 8:(grp + 1) * 8,
                            qt * 128:(qt + 1) * 128],
                        in_=pTb.ap().rearrange("p (j q) -> p j q", j=8)))
                bdone(pTb, s_ev, s_ev.v)
        probT_ev = s_ev.v

        for dg in range(2):
            for mj in range(32):
                tb = sTb[mj % 2]

                ts_sem = (s_tA, s_tB)[mj % 2]

                def _stld(s, bsel, par, dg=dg, mj=mj, tb=tb, ts_sem=ts_sem,
                          pu=sT_free[mj % 2]):
                    if pu:
                        s.wait_ge(s_mm.h, pu)
                    h = par if mj < 16 else 1 - par
                    lm = (mj % 16) * 128
                    s.dma_start_transpose(
                        tb[:, 0:512],
                        sh_s[h, dg * 512:(dg + 1) * 512,
                             lm:lm + 128]).then_inc(ts_sem.h, 16)
                dsync(_stld)
                ts_sem.v += 16
                tld = ts_sem.v

                @block.tensor
                def _(t, dg=dg, mj=mj, tb=tb, val=tld, pv=probT_ev,
                      ts_sem=ts_sem):
                    t.wait_ge(ts_sem.h, val)
                    if mj == 0:
                        t.wait_ge(s_ev.h, pv)
                    for dd in range(4):
                        pt = (pA, pB, pC, pD)[dd]
                        if mj == 0:
                            bwait(t, pt)
                        mm = t.matmul(
                            pt[:, :],
                            tb[:, dd * 128:(dd + 1) * 128],
                            sprobT.ap().rearrange("p (mj q) -> p mj q",
                                                  mj=32)[:, mj, :],
                            start=(mj == 0), stop=(mj == 31),
                            skip_group_check=True)
                    s_mm.inc(mm)
                sT_free[mj % 2] = s_mm.v
            pv_mm = s_mm.v
            prev_pv_mm = pv_mm

            ep_base = s_ev.v

            @block.vector
            def _(v, ng=ng, dg=dg, val=pv_mm, gv=glob_ev, base=ep_base):
                v.wait_ge(s_mm.h, val)
                v.wait_ge(s_ev.h, gv)
                for dd in range(4):
                    d = dg * 4 + dd
                    pt = (pA, pB, pC, pD)[dd]
                    s_ev.inc(v.scalar_tensor_tensor(
                        out=ep1[:, :], in0=pt[:, :], scalar=1.0,
                        in1=sfm.ap().rearrange("p (sig m) -> p sig m",
                                               sig=8)[
                            :, d, ng * 512:(ng + 1) * 512],
                        op0=ALU.mult, op1=ALU.add))
                    v.wait_ge(s_ev.h, base + 2 * dd + 1)
                    s_ev.inc(v.tensor_scalar(
                        out=sloc[:, bass.ts(dd, TC)], in0=ep1[:, :],
                        scalar1=sglob[:, d:d + 1], scalar2=sbeta[:, 0:1],
                        op0=ALU.add, op1=ALU.mult))
            ep_ev = s_ev.v
            for dd in range(4):
                bdone((pA, pB, pC, pD)[dd], s_ev, ep_ev)

            @block.scalar
            def _(sc, val=ep_ev):
                sc.wait_ge(s_ev.h, val)
                s_sc.inc(sc.activation(out=sloc[:, :], in_=sloc[:, :],
                                       func=AF.Tanh))
            tanh_sc = s_sc.v

            def _obst(s, bsel, par, ng=ng, dg=dg, val=tanh_sc):
                s.wait_ge(s_sc.h, val)
                s.dma_start(
                    out=dr_ob.rearrange("p (sl sig q) -> p sl sig q",
                                        sl=4, sig=8)[
                        :, ng, dg * 4:(dg + 1) * 4, :],
                    in_=sloc.ap().rearrange("p (dd q) -> p dd q", dd=4)
                ).then_inc(s_ob.h, 16)
            dsync(_obst)
            s_ob.v += 16
            sloc_guard = s_ob.v

            @block.vector
            def _(v, val=sloc_guard):
                v.wait_ge(s_ob.h, val)

    flush_sync()
    barrier()
    if PHASE_LIMIT <= 5:
        return debug_finish()
    scC.close()
    scHA.close()

    scD = Scope()
    X = scD.SB("X2", [128, 16384], F32)
    hA = scD.SB("hA2", [128, 16384], BF16)
    hB = scD.SB("hB2", [128, 16384], BF16)
    wbuf = [scD.SB(f"wbuf2{i}", [128, 4096], BF16) for i in range(2)]
    spo = scD.SB("spo2", [128, 4096], F32)
    fst1 = scD.SB("fst12", [128, 512], F32)
    pR1 = scD.PSUM("pR1d", [1, 512])
    pR2 = scD.PSUM("pR2d", [1, 512])
    souttm = spo
    wbuf_free = [0, 0]

    sln_free_D = 0
    spo_free_D = 0
    prev_store = 0
    for sl in range(4):
        @block.sync
        def _(s, sl=sl, spf=spo_free_D, pst=prev_store):
            if spf:
                s.wait_ge(s_ev.h, spf)
            if pst:
                s.wait_ge(s_st.h, pst)
            s_xld.inc(s.dma_start(out=X[:, bass.ts(sl, 4096)],
                                  in_=dr_x2[:, bass.ts(sl, 4096)]), 16)
            s_xld.inc(s.dma_start(out=spo[:, :],
                                  in_=dr_ob[:, bass.ts(sl, 4096)]), 16)
        x3_ld = s_xld.v

        @block.vector
        def _(v, sl=sl, val=x3_ld):
            v.wait_ge(s_xld.h, val)
            s_ev.inc(v.tensor_tensor(out=X[:, bass.ts(sl, 4096)],
                                     in0=X[:, bass.ts(sl, 4096)],
                                     in1=spo[:, :], op=ALU.add))
        x3_ev = s_ev.v
        spo_free_D = x3_ev

        lnv = emit_ln(X.ap()[:, bass.ts(sl, 4096)], s_ev, x3_ev, pR1, pR2,
                      extra_wr=[(s_mm, sln_free_D)])
        ffn_done, sln_free_D = emit_ffn(sl * 4096, lnv, wbuf, fst1, hA, hB, X)

        otr_base = s_ev.v

        @block.tensor
        def _(t, sl=sl, val=ffn_done, ob=otr_base):
            t.wait_ge(s_ev.h, val)
            for tt in range(4):
                for half in range(2):
                    k = tt * 2 + half
                    pt = (pA, pB, pE, pF)[k % 4]
                    bwait(t, pt)
                    if k >= 4:
                        t.wait_ge(s_ev.h, ob + k - 3)
                    for j in range(4):
                        sig = half * 4 + j
                        mm = t.transpose(
                            pt.ap().rearrange("p (j q) -> p j q", j=4)[
                                :, j, :],
                            X[:, sl * 4096 + sig * TC + tt * 128:
                              sl * 4096 + sig * TC + tt * 128 + 128],
                            s_eye[:, :])
                    s_mm.inc(mm)
        otr_mms = [s_mm.v - 8 + i + 1 for i in range(8)]

        @block.vector
        def _(v, sl=sl, vals=otr_mms):
            k = 0
            for tt in range(4):
                for half in range(2):
                    pt = (pA, pB, pE, pF)[(tt * 2 + half) % 4]
                    v.wait_ge(s_mm.h, vals[k])
                    s_ev.inc(v.tensor_copy(
                        out=souttm[:, tt * 1024 + half * 512:
                                   tt * 1024 + (half + 1) * 512],
                        in_=pt[:, :]))
                    k += 1
        oev = [s_ev.v - 8 + i + 1 for i in range(8)]
        for i, pt in enumerate([pA, pB, pE, pF]):
            bdone(pt, s_ev, oev[4 + i])

        @block.sync
        def _(s, sl=sl, val=oev[-1]):
            s.wait_ge(s_ev.h, val)
            s_st.inc(s.dma_start(
                out=out_tok.rearrange("(sl tt p) d -> sl p tt d", sl=4,
                                      tt=4)[sl, :, :, :],
                in_=souttm.ap().rearrange("p (tt d) -> p tt d", tt=4)), 16)
        prev_store = s_st.v

    @block.sync
    def _(s):
        s_st.inc(s.dma_start(out=out_id[:, :], in_=sid[:, :]), 16)
        s.wait_ge(s_st.h, s_st.v)

    scD.close()
    es.close()
    return nc


def kernel(**inputs):
    ins = {k: np.asarray(v) for k, v in inputs.items()}
    pre = {
        "x0": np.ascontiguousarray(ins["L0_attn_a_hidden_states"], np.float32),
        "W_in": np.ascontiguousarray(ins["L0_attn_a_W_in"], np.float32),
        "conv_w": np.ascontiguousarray(
            ins["L0_attn_a_conv_w"].T.reshape(2, 8, 128, 4)
            .transpose(0, 2, 1, 3).reshape(2, 128, 32), np.float32),
        "conv_b": np.ascontiguousarray(
            ins["L0_attn_a_conv_b"].reshape(2, 8, 128)
            .transpose(0, 2, 1), np.float32),
        "W_x": np.ascontiguousarray(ins["L0_attn_a_W_x"], np.float32),
        "W_dt": np.ascontiguousarray(ins["L0_attn_a_W_dt"], np.float32),
        "b_dt": np.ascontiguousarray(
            ins["L0_attn_a_b_dt"].reshape(2, 8, 128)
            .transpose(0, 2, 1), np.float32),
        "A_log": np.ascontiguousarray(
            ins["L0_attn_a_A_log"].reshape(2, 8, 128, 16)
            .transpose(0, 2, 1, 3).reshape(2, 128, 128), np.float32),
        "D_vec": np.ascontiguousarray(
            ins["L0_attn_a_D"].reshape(2, 8, 128)
            .transpose(0, 2, 1), np.float32),
        "W_out": np.ascontiguousarray(ins["L0_attn_a_W_out"], np.float32),
        "W_spinT": np.ascontiguousarray(ins["L0_attn_b_W_spin"].T, np.float32),
        "b_spin": np.ascontiguousarray(
            ins["L0_attn_b_b_spin"].reshape(8, 128).T, np.float32),
        "W_globT": np.ascontiguousarray(ins["L0_attn_b_W_global"].T,
                                        np.float32),
        "W_JT": np.ascontiguousarray(ins["L0_attn_b_W_J"].T, np.float32),
        "beta": np.ascontiguousarray(
            ins["L0_attn_b_beta"].reshape(1, 1), np.float32),
        "w1T": np.ascontiguousarray(ins["L0_ffn_w1_w"].T, np.float32),
        "w1s": np.ascontiguousarray(
            ins["L0_ffn_w1_s"].reshape(32, 128).T, np.float32),
        "w2T": np.ascontiguousarray(ins["L0_ffn_w2_w"].T, np.float32),
        "w2s": np.ascontiguousarray(
            ins["L0_ffn_w2_s"].reshape(32, 128).T, np.float32),
        "h1T": np.ascontiguousarray(ins["L0_ffn_h1_w"].T, np.float32),
        "h1s": np.ascontiguousarray(
            ins["L0_ffn_h1_s"].reshape(32, 128).T, np.float32),
        "h2T": np.ascontiguousarray(ins["L0_ffn_h2_w"].T, np.float32),
        "h2s": np.ascontiguousarray(
            ins["L0_ffn_h2_s"].reshape(32, 128).T, np.float32),
        "w3T": np.ascontiguousarray(ins["L0_ffn_w3_w"].T, np.float32),
        "w3s": np.ascontiguousarray(
            ins["L0_ffn_w3_s"].reshape(8, 128).T, np.float32),
        "eye": np.eye(128, dtype=np.float32),
        "constv": np.ascontiguousarray(
            np.tile(np.array([1e-5, 1.0, -0.5, -1.0, 1.0 / 32.0, 0.5, 0.0,
                              2.0], np.float32), (128, 1))),
    }
    nc = build_program()
    res = run_bass_kernel_spmd(nc, [dict(pre) for _ in range(8)],
                               core_ids=list(range(8)))
    global LAST_RESULTS
    LAST_RESULTS = res.results
    out = np.zeros((NB, T, D), np.float32)
    for c in range(8):
        r = res.results[c]
        ids = r["out_id"][0]
        b, par = int(ids[0]), int(ids[1])
        out[b, par * 2048:(par + 1) * 2048, :] = r["out_tok"]
    return out


# version 04
# speedup vs baseline: 1.8641x; 1.8641x over previous
import numpy as np
from contextlib import ExitStack

import concourse.bass as bass
import concourse.mybir as mybir
from concourse.bass_utils import run_bass_kernel_spmd
from concourse.expressions_rust import RuntimeValue

F32 = mybir.dt.float32
BF16 = mybir.dt.bfloat16
I32 = mybir.dt.int32
AF = mybir.ActivationFunctionType
ALU = mybir.AluOpType
AX = mybir.AxisListType

D = 1024
T = 4096
NB = 4
TC = 512
NCH = 8
NS = 16
DHID = 4096
EPS = 1e-5
POLL_CAP = 20_000_000
PHASE_LIMIT = 99
DEBUG_DUMP = None
DEBUG_ATTN = False
LAST_RESULTS = None


class Sem:
    def __init__(self, handle):
        self.h = handle
        self.v = 0

    def inc(self, instr, n=1):
        instr.then_inc(self.h, n)
        self.v += n
        return self.v


def build_program():
    nc = bass.Bass("TRN2", target_bir_lowering=False)
    es = ExitStack()
    ap = lambda t: t.ap()

    x0_in = ap(nc.dram_tensor("x0", [NB, T, D], F32, kind="ExternalInput"))
    W_in = ap(nc.dram_tensor("W_in", [D, 4096], F32, kind="ExternalInput"))
    conv_w = ap(nc.dram_tensor("conv_w", [2, 128, 32], F32, kind="ExternalInput"))
    conv_b = ap(nc.dram_tensor("conv_b", [2, 128, 8], F32, kind="ExternalInput"))
    W_x = ap(nc.dram_tensor("W_x", [2048, 96], F32, kind="ExternalInput"))
    W_dt = ap(nc.dram_tensor("W_dt", [64, 2048], F32, kind="ExternalInput"))
    b_dt = ap(nc.dram_tensor("b_dt", [2, 128, 8], F32, kind="ExternalInput"))
    A_log = ap(nc.dram_tensor("A_log", [2, 128, 128], F32, kind="ExternalInput"))
    D_vec = ap(nc.dram_tensor("D_vec", [2, 128, 8], F32, kind="ExternalInput"))
    W_out = ap(nc.dram_tensor("W_out", [2048, D], F32, kind="ExternalInput"))
    W_spinT = ap(nc.dram_tensor("W_spinT", [D, D], F32, kind="ExternalInput"))
    b_spin = ap(nc.dram_tensor("b_spin", [128, 8], F32, kind="ExternalInput"))
    W_globT = ap(nc.dram_tensor("W_globT", [D, D], F32, kind="ExternalInput"))
    W_JT = ap(nc.dram_tensor("W_JT", [D, D], F32, kind="ExternalInput"))
    beta_in = ap(nc.dram_tensor("beta", [1, 1], F32, kind="ExternalInput"))
    w1T = ap(nc.dram_tensor("w1T", [D, DHID], F32, kind="ExternalInput"))
    w1s = ap(nc.dram_tensor("w1s", [128, 32], F32, kind="ExternalInput"))
    w2T = ap(nc.dram_tensor("w2T", [D, DHID], F32, kind="ExternalInput"))
    w2s = ap(nc.dram_tensor("w2s", [128, 32], F32, kind="ExternalInput"))
    h1T = ap(nc.dram_tensor("h1T", [DHID, DHID], F32, kind="ExternalInput"))
    h1s = ap(nc.dram_tensor("h1s", [128, 32], F32, kind="ExternalInput"))
    h2T = ap(nc.dram_tensor("h2T", [DHID, DHID], F32, kind="ExternalInput"))
    h2s = ap(nc.dram_tensor("h2s", [128, 32], F32, kind="ExternalInput"))
    w3T = ap(nc.dram_tensor("w3T", [DHID, D], F32, kind="ExternalInput"))
    w3s = ap(nc.dram_tensor("w3s", [128, 8], F32, kind="ExternalInput"))
    eye_in = ap(nc.dram_tensor("eye", [128, 128], F32, kind="ExternalInput"))
    const_in = ap(nc.dram_tensor("constv", [128, 8], F32,
                                 kind="ExternalInput"))

    out_tok = ap(nc.dram_tensor("out_tok", [2048, D], F32, kind="ExternalOutput"))
    out_id = ap(nc.dram_tensor("out_id", [1, 4], I32, kind="ExternalOutput"))

    sh_proj = ap(nc.dram_tensor("sh_proj", [2, NCH, 96, TC], F32,
                                addr_space="Shared"))
    sh_pout = ap(nc.dram_tensor("sh_pout", [2, NCH, D, TC], F32,
                                addr_space="Shared"))
    sh_s = ap(nc.dram_tensor("sh_s", [2, D, 2048], BF16, addr_space="Shared"))
    sh_flag = ap(nc.dram_tensor("sh_flag", [2, 32, 512], I32,
                                addr_space="Shared"))
    sh_mean = ap(nc.dram_tensor("sh_mean", [2, 128, 8], F32,
                                addr_space="Shared"))
    dr_x2 = ap(nc.dram_tensor("dr_x2", [128, 16384], F32))
    dr_ob = ap(nc.dram_tensor("dr_ob", [128, 16384], F32))
    dr_dbg = ap(nc.dram_tensor("dr_dbg", [128, 16384], F32))
    dw1c = ap(nc.dram_tensor("dw1c", [32, 128, 1024], BF16))
    dw2c = ap(nc.dram_tensor("dw2c", [32, 128, 1024], BF16))
    dw3c = ap(nc.dram_tensor("dw3c", [8, 128, 4096], BF16))
    dh1c = ap(nc.dram_tensor("dh1c", [32, 128, 4096], BF16))
    dh2c = ap(nc.dram_tensor("dh2c", [32, 128, 4096], BF16))

    def SEM(name):
        return Sem(es.enter_context(nc.semaphore(name)))

    s_ld = SEM("s_ld")
    s_wld = SEM("s_wld")
    s_mm = SEM("s_mm")
    s_ev = SEM("s_ev")
    s_sc = SEM("s_sc")
    s_gp = SEM("s_gp")
    s_pl = SEM("s_pl")
    s_st = SEM("s_st")
    s_x0 = SEM("s_x0")
    s_xch = SEM("s_xch")
    s_wA = SEM("s_wA")
    s_wB = SEM("s_wB")
    s_xld = SEM("s_xld")
    s_ob = SEM("s_ob")
    s_tA = SEM("s_tA")
    s_tB = SEM("s_tB")
    s_sel = SEM("s_sel")
    s_stg = SEM("s_stg")
    s_cl = SEM("s_cl")
    s_cs = SEM("s_cs")

    def SB(name, shape, dt):
        return es.enter_context(nc.sbuf_tensor(name, shape, dt))

    def PSUM(name, shape, dt=F32):
        return es.enter_context(nc.psum_tensor(name, shape, dt))

    open_scopes = []

    class Scope:
        def __init__(self):
            self.es = ExitStack()
            open_scopes.append(self)

        def SB(self, name, shape, dt, side="left"):
            return self.es.enter_context(
                nc.sbuf_tensor(name, shape, dt, side=side))

        def PSUM(self, name, shape, dt=F32):
            return self.es.enter_context(nc.psum_tensor(name, shape, dt))

        def close(self):
            if self in open_scopes:
                open_scopes.remove(self)
            self.es.close()

    onesr_b = SB("onesr_b", [1, 128], BF16)
    onesr_f = SB("onesr_f", [1, 128], F32)
    onesc_f = SB("onesc_f", [128, 1], F32)
    s_eye = SB("s_eye", [128, 128], F32)
    s_eye_b = SB("s_eye_b", [128, 128], BF16)
    sid = SB("sid", [1, 4], I32)
    sflag1 = SB("sflag1", [1, 1], I32)
    sfpoll = SB("sfpoll", [1, 1], I32)
    hstate = SB("hstate", [128, 128], F32)
    sA = SB("sA", [128, 128], F32)
    sconvw = SB("sconvw", [128, 32], F32)
    sconvb = SB("sconvb", [128, 8], F32)
    sbdt = SB("sbdt", [128, 8], F32)
    sDv = SB("sDv", [128, 8], F32)
    sw1s = SB("sw1s", [128, 32], F32)
    sw2s = SB("sw2s", [128, 32], F32)
    sh1s = SB("sh1s", [128, 32], F32)
    sh2s = SB("sh2s", [128, 32], F32)
    sw3s = SB("sw3s", [128, 8], F32)
    sbspin = SB("sbspin", [128, 8], F32)
    sbeta = SB("sbeta", [128, 1], F32)
    r_mean = SB("r_mean", [1, TC], F32)
    r_var = SB("r_var", [1, TC], F32)
    r_rstd = SB("r_rstd", [1, TC], F32)
    r_mr = SB("r_mr", [1, TC], F32)
    sbc = SB("sbc", [128, 2 * TC], F32)
    ffv = SB("ffv", [1, 1], F32)
    ffs = SB("ffs", [1, 1], F32)
    ffg = SB("ffg", [1, 1], F32)
    scrA = SB("scrA", [128, 4096], F32)
    sln = SB("sln", [128, 4096], BF16)
    sconst = SB("sconst", [128, 8], F32)
    smean = SB("smean", [128, 8], F32)
    smean_p = SB("smean_p", [128, 8], F32)
    smean_b = SB("smean_b", [128, 8], BF16)
    sglob = SB("sglob", [128, 8], F32)

    pA = PSUM("pA", [128, 512])
    pB = PSUM("pB", [128, 512])
    pC = PSUM("pC", [128, 512])
    pD = PSUM("pD", [128, 512])
    pE = PSUM("pE", [128, 512])
    pF = PSUM("pF", [128, 512])

    CONSTS = [EPS, 1.0, -0.5, -1.0, 1.0 / 32.0, 0.5, 0.0, 2.0]
    for i, cv in enumerate(CONSTS):
        nc.const_aps.register(F32, cv, sconst[:, i:i + 1])

    block = es.enter_context(nc.Block())
    regs = {}

    sync_q = []

    def dsync(fn):
        sync_q.append(fn)

    def flush_sync():
        if not sync_q:
            return
        q = list(sync_q)
        sync_q.clear()

        @block.sync
        def _(s, q=q):
            idx = RuntimeValue(regs["ridx"], 0, 7)
            for i in nc.Switch(engines=[nc.sync], index=[idx], n=8):
                bsel, par = i >> 1, i & 1
                for fn in q:
                    fn(s, bsel, par)

    bank_free = {}

    def bwait(t, bank):
        info = bank_free.get(id(bank))
        if info:
            t.wait_ge(info[0].h, info[1])

    def bdone(bank, sem, val):
        bank_free[id(bank)] = (sem, val)

    def barrier():
        @block.vector
        def _(v):
            s_ev.inc(v.memset(ffv[:, :], 0.0))

        @block.scalar
        def _(sc):
            s_sc.inc(sc.copy(ffs[:, :], sconst[0:1, 0:1]))

        @block.gpsimd
        def _(g):
            s_gp.inc(g.memset(ffg[:, :], 0.0))

        waits = [(s_ld, s_ld.v), (s_wld, s_wld.v), (s_ev, s_ev.v),
                 (s_sc, s_sc.v), (s_gp, s_gp.v), (s_mm, s_mm.v),
                 (s_x0, s_x0.v), (s_xch, s_xch.v), (s_wA, s_wA.v),
                 (s_wB, s_wB.v), (s_xld, s_xld.v), (s_ob, s_ob.v),
                 (s_tA, s_tA.v), (s_tB, s_tB.v), (s_sel, s_sel.v),
                 (s_stg, s_stg.v), (s_cl, s_cl.v), (s_cs, s_cs.v),
                 (s_st, s_st.v)]

        for name in ["sync", "gpsimd", "vector", "scalar", "tensor"]:
            @getattr(block, name)
            def _(e, waits=waits):
                for sem, val in waits:
                    if val > 0:
                        e.wait_ge(sem.h, val)

    setup = Scope()
    wstage = setup.SB("wstage", [128, 8192], F32, side="right")

    scA = Scope()
    sWinX = scA.SB("sWinX", [128, 8192], BF16)
    sWinZ = scA.SB("sWinZ", [128, 8192], BF16)
    sWxT = scA.SB("sWxT", [128, 8 * 96], BF16)
    sWdt = scA.SB("sWdt", [64, 1024], BF16)
    sWout = scA.SB("sWout", [128, 8192], BF16)

    @block.sync
    def _(s):
        for nm in ["rphys", "ridx", "rflag", "rcond", "riter"]:
            regs[nm] = s.alloc_register(nm)
        rphys, ridx = regs["rphys"], regs["ridx"]
        for arm in nc.switch_core_id(engines=[nc.sync], n=128):
            s.reg_mov(rphys, int(arm))
        s.reg_alu(ridx, rphys, 7, ALU.bitwise_and)
        s.reg_save(sid[0:1, 2:3], rphys)
        s.reg_save(sid[0:1, 3:4], 7)
        s_ld.inc(s.dma_start(out=s_eye[:, :], in_=eye_in[:, :]), 16)
        s_ld.inc(s.dma_start(out=sconst[:, :], in_=const_in[:, :]), 16)
        for srcw, dst in [(w1s, sw1s), (w2s, sw2s), (h1s, sh1s), (h2s, sh2s),
                          (w3s, sw3s), (b_spin, sbspin)]:
            s_ld.inc(s.dma_start(out=dst[:, :], in_=srcw[:, :]), 16)
        s_ld.inc(s.dma_start(out=sbeta[0:1, 0:1], in_=beta_in[:, :]), 16)

    def _param_loads(s, bsel, par):
        s.reg_save(sid[0:1, 0:1], bsel)
        s.reg_save(sid[0:1, 1:2], par)
        s.dma_start(out=sA[:, :], in_=A_log[par, :, :]).then_inc(s_ld.h, 16)
        s.dma_start(out=sconvw[:, :],
                    in_=conv_w[par, :, :]).then_inc(s_ld.h, 16)
        for vec, dst in [(conv_b, sconvb), (b_dt, sbdt), (D_vec, sDv)]:
            s.dma_start(out=dst[:, :],
                        in_=vec[par, :, :]).then_inc(s_ld.h, 16)
    dsync(_param_loads)
    s_ld.v += 16 * 5
    PARAM_LD = s_ld.v

    @block.vector
    def _(v):
        v.memset(onesr_b[:, :], 1.0)
        v.memset(onesr_f[:, :], 1.0)
        v.memset(onesc_f[:, :], 1.0)
        v.memset(hstate[:, :], 0.0)
        s_ev.inc(v.memset(sflag1[:, :], 1))
    MEMSET_EV = s_ev.v

    def stage(load_fn, cast_fn, ndma):
        prev = s_ev.v

        def _ld(s, bsel, par, prev=prev, load_fn=load_fn):
            if prev > MEMSET_EV:
                s.wait_ge(s_ev.h, prev)
            load_fn(s, par)
        dsync(_ld)
        s_stg.v += 16 * ndma
        tgt = s_stg.v

        @block.vector
        def _(v, tgt=tgt):
            v.wait_ge(s_stg.h, tgt)
            s_ev.inc(cast_fn(v))

    def ld_winx(s, par):
        s.dma_start(
            out=wstage.ap().rearrange("p (sig m q) -> p sig m q", sig=8, m=8),
            in_=W_in.rearrange("(sig p) (h m q) -> h p sig m q", sig=8, h=4,
                               m=8)[par, :, :, :, :]).then_inc(s_stg.h, 16)

    def ld_winz(s, par):
        s.dma_start(
            out=wstage.ap().rearrange("p (sig m q) -> p sig m q", sig=8, m=8),
            in_=W_in.rearrange("(sig p) (h m q) -> h p sig m q", sig=8, h=4,
                               m=8)[par + 2, :, :, :, :]).then_inc(s_stg.h, 16)

    def ld_wout(s, par):
        s.dma_start(
            out=wstage.ap().rearrange("p (mu m q) -> p mu m q", mu=8, m=8),
            in_=W_out.rearrange("(h mu p) (m q) -> h p mu m q", h=2, mu=8,
                                m=8)[par, :, :, :, :]).then_inc(s_stg.h, 16)

    def ld_wx_wdt(s, par):
        s.dma_start(
            out=wstage.ap().rearrange("p (mu j) -> p mu j", mu=8)[:, :, 0:96],
            in_=W_x.rearrange("(h mu p) j -> h p mu j", h=2, mu=8)[
                par, :, :, :]).then_inc(s_stg.h, 16)
        s.dma_start(
            out=wstage.ap().rearrange("p (mu j) -> p mu j", mu=8)[
                0:64, :, 128:256],
            in_=W_dt.rearrange("r (h mu q) -> h r mu q", h=2, mu=8)[
                par, :, :, :]).then_inc(s_stg.h, 16)

    stage(ld_winx, lambda v: v.tensor_copy(out=sWinX[:, :], in_=wstage[:, :]),
          1)
    stage(ld_winz, lambda v: v.tensor_copy(out=sWinZ[:, :], in_=wstage[:, :]),
          1)
    stage(ld_wout, lambda v: v.tensor_copy(out=sWout[:, :], in_=wstage[:, :]),
          1)

    def cast_wx_wdt(v):
        v.tensor_copy(out=sWxT.ap().rearrange("p (mu j) -> p mu j", mu=8),
                      in_=wstage.ap().rearrange("p (mu j) -> p mu j", mu=8)[
                          :, :, 0:96])
        return v.tensor_copy(
            out=sWdt.ap().rearrange("p (mu q) -> p mu q", mu=8),
            in_=wstage.ap().rearrange("p (mu j) -> p mu j", mu=8)[
                0:64, :, 128:256])

    stage(ld_wx_wdt, cast_wx_wdt, 2)

    @block.vector
    def _(v, tgt=PARAM_LD):
        v.wait_ge(s_ld.h, tgt)
        s_ev.inc(v.tensor_copy(out=s_eye_b[:, :], in_=s_eye[:, :]))

    @block.scalar
    def _(sc, tgt=PARAM_LD):
        sc.wait_ge(s_ld.h, tgt)
        s_sc.inc(sc.activation(out=sA[:, :], in_=sA[:, :], func=AF.Exp))
    A_SC = s_sc.v

    @block.vector
    def _(v, tgt=A_SC):
        v.wait_ge(s_sc.h, tgt)
        s_ev.inc(v.tensor_scalar_mul(sA[:, :], sA[:, :], -1.0))

    @block.tensor
    def _(t, a=PARAM_LD, b=MEMSET_EV):
        t.wait_ge(s_ld.h, a)
        t.wait_ge(s_ev.h, b)
        s_mm.inc(t.matmul(pA[:, 0:1], onesr_f[:, :], sbeta[0:1, 0:1],
                          start=True, stop=True))
    BETA_MM = s_mm.v

    @block.vector
    def _(v, tgt=BETA_MM):
        v.wait_ge(s_mm.h, tgt)
        s_ev.inc(v.tensor_copy(out=sbeta[:, :], in_=pA[:, 0:1]))
    bdone(pA, s_ev, s_ev.v)

    setup.close()

    x0tm = scA.SB("x0tm", [128, 1024], F32)
    x0fm = scA.SB("x0fm", [128, 4096], F32)
    xcat = [scA.SB("xcat0", [128, 8 * 515], BF16)]
    ctx3 = scA.SB("ctx3", [128, 24], BF16)
    sxs = scA.SB("sxs", [128, 4096], BF16)
    sdt = scA.SB("sdt", [128, 4096], BF16)
    sdtx = scA.SB("sdtx", [128, 4096], BF16)
    szs = scA.SB("szs", [128, 4096], BF16)
    sproj = scA.SB("sproj", [96, TC], F32)
    sBC32 = scA.SB("sBC32", [32, TC], BF16)
    sel = scA.SB("sel", [32, 32 * 128], BF16)
    sBb = scA.SB("sBb", [128, TC], BF16)
    sCb = scA.SB("sCb", [128, TC], BF16)
    sa_t = [scA.SB(f"sa_t{i}", [128, TC], F32) for i in range(2)]
    su_t = [scA.SB(f"su_t{i}", [128, TC], F32) for i in range(2)]
    shh = [scA.SB(f"shh{i}", [128, TC], F32) for i in range(2)]
    syp = scA.SB("syp", [128, TC], F32)
    pRa_A = scA.PSUM("pRa_A", [1, 512])
    pRb_A = scA.PSUM("pRb_A", [1, 512])
    cstg = [scA.SB(f"cstg{i}", [128, 4096], BF16) for i in range(2)]

    cast_jobs = []
    for src, dst, nsig in [(w1T, dw1c, 8), (w2T, dw2c, 8)]:
        for m in range(32):
            cast_jobs.append((
                lambda g, b, m=m, src=src: g.dma_start(
                    out=b.ap()[:, 0:1024].rearrange("p (sig q) -> p sig q",
                                                    sig=8),
                    in_=src.rearrange("(sig p) (m q) -> p sig m q", sig=8,
                                      m=32)[:, :, m, :]),
                lambda g, b, m=m, dst=dst: g.dma_start(out=dst[m, :, :],
                                                       in_=b[:, 0:1024])))
    for m in range(8):
        cast_jobs.append((
            lambda g, b, m=m: g.dma_start(
                out=b.ap().rearrange("p (sig q) -> p sig q", sig=32),
                in_=w3T.rearrange("(sig p) (m q) -> p sig m q", sig=32,
                                  m=8)[:, :, m, :]),
            lambda g, b, m=m: g.dma_start(out=dw3c[m, :, :], in_=b[:, :])))
    for hT, dst in [(h1T, dh1c), (h2T, dh2c)]:
        for m in range(32):
            cast_jobs.append((
                lambda g, b, m=m, hT=hT: g.dma_start(
                    out=b.ap().rearrange("p (sig q) -> p sig q", sig=32),
                    in_=hT.rearrange("(sig p) (m q) -> p sig m q", sig=32,
                                     m=32)[:, :, m, :]),
                lambda g, b, m=m, dst=dst: g.dma_start(out=dst[m, :, :],
                                                       in_=b[:, :])))
    cast_idx = 0

    def emit_cast_block(njobs):
        nonlocal cast_idx
        i0 = cast_idx
        jobs = cast_jobs[i0:i0 + njobs]
        cast_idx += len(jobs)
        if not jobs:
            return

        @block.gpsimd
        def _(g, jobs=jobs, i0=i0):
            for k, (ld, st) in enumerate(jobs):
                i = i0 + k
                if i >= 2:
                    g.wait_ge(s_cs.h, 16 * (i - 1))
                s_cl.inc(ld(g, cstg[i % 2]), 16)
                g.wait_ge(s_cl.h, s_cl.v)
                s_cs.inc(st(g, cstg[i % 2]), 16)

    def debug_finish():
        flush_sync()

        @block.sync
        def _(s):
            for sem in [s_ld, s_wld, s_x0, s_xch, s_wA, s_wB, s_xld, s_ob,
                        s_tA, s_tB, s_sel, s_stg, s_cl, s_cs]:
                if sem.v:
                    s.wait_ge(sem.h, sem.v)
            if DEBUG_DUMP:
                dsrc = {"dr_x2": dr_x2, "dr_ob": dr_ob, "dr_dbg": dr_dbg}[DEBUG_DUMP]
                s_st.inc(s.dma_start(
                    out=out_tok.rearrange("(a p) d -> p a d", a=16),
                    in_=dsrc.rearrange("p (a d) -> p a d", a=16)), 16)
            s_st.inc(s.dma_start(out=out_id[:, :], in_=sid[:, :]), 16)
            s.wait_ge(s_st.h, s_st.v)
        for sc_ in list(reversed(open_scopes)):
            sc_.close()
        es.close()
        return nc

    flush_sync()

    @block.vector
    def _(v):
        v.memset(ctx3[:, :], 0.0)
        v.memset(sel[:, :], 0.0)
        s_ev.inc(v.memset(xcat[0][:, :], 0.0))
    MEMSET2_EV = s_ev.v

    @block.sync
    def _(s, val=MEMSET2_EV):
        s.wait_ge(s_ev.h, val)
        for k in range(32):
            s_sel.inc(s.dma_start(out=sel[k:k + 1, k * 128:(k + 1) * 128],
                                  in_=onesr_b[:, :]), 16)
    SEL_LD = s_sel.v

    barrier()
    if PHASE_LIMIT <= 1:
        return debug_finish()

    def emit_ln(src_fm, ready_sem, ready_val, bra, brb, extra_sq=(),
                extra_wr=()):
        @block.scalar
        def _(sc, sem=ready_sem, val=ready_val, ex=tuple(extra_sq)):
            sc.wait_ge(sem.h, val)
            for xs_, xv_ in ex:
                if xv_:
                    sc.wait_ge(xs_.h, xv_)
            s_sc.inc(sc.activation(out=scrA[:, :], in_=src_fm[:, :],
                                   func=AF.Square))
        sq_done = s_sc.v

        @block.tensor
        def _(t, val=sq_done, bra=bra, brb=brb):
            t.wait_ge(s_sc.h, val)
            bwait(t, bra)
            bwait(t, brb)
            for sig in range(8):
                t.matmul(bra[:, :], onesc_f[:, :], src_fm[:, bass.ts(sig, TC)],
                         start=(sig == 0), stop=(sig == 7))
            for sig in range(8):
                mm = t.matmul(brb[:, :], onesc_f[:, :],
                              scrA[:, bass.ts(sig, TC)],
                              start=(sig == 0), stop=(sig == 7))
            s_mm.inc(mm)
        red_done = s_mm.v

        base = s_ev.v

        @block.vector
        def _(v, val=red_done, base=base, bra=bra, brb=brb):
            v.wait_ge(s_mm.h, val)
            v.tensor_scalar_mul(r_mean[:, :], bra[:, :], 1.0 / D)
            s_ev.inc(v.tensor_scalar_mul(r_var[:, :], brb[:, :], 1.0 / D))
            v.wait_ge(s_ev.h, base + 1)
            s_ev.inc(v.tensor_tensor(out=r_mr[:, :], in0=r_mean[:, :],
                                     in1=r_mean[:, :], op=ALU.mult))
            v.wait_ge(s_ev.h, base + 2)
            s_ev.inc(v.tensor_tensor(out=r_var[:, :], in0=r_var[:, :],
                                     in1=r_mr[:, :], op=ALU.subtract))
        var_done = s_ev.v
        bdone(bra, s_ev, var_done)
        bdone(brb, s_ev, var_done)

        base2 = s_sc.v

        @block.scalar
        def _(sc, val=var_done, base2=base2):
            sc.wait_ge(s_ev.h, val)
            s_sc.inc(sc.activation(out=r_rstd[:, :], in_=r_var[:, :],
                                   func=AF.Ln, bias=EPS))
            sc.wait_ge(s_sc.h, base2 + 1)
            s_sc.inc(sc.activation(out=r_rstd[:, :], in_=r_rstd[:, :],
                                   func=AF.Exp, scale=-0.5))
        rstd_done = s_sc.v

        @block.vector
        def _(v, val=rstd_done):
            v.wait_ge(s_sc.h, val)
            s_ev.inc(v.tensor_tensor(out=r_mr[:, :], in0=r_mean[:, :],
                                     in1=r_rstd[:, :], op=ALU.mult))
        mr_done = s_ev.v

        @block.tensor
        def _(t, val=mr_done):
            t.wait_ge(s_ev.h, val)
            bwait(t, pE)
            bwait(t, pF)
            t.matmul(pE[:, :], onesr_f[:, :], r_rstd[:, :], start=True,
                     stop=True)
            s_mm.inc(t.matmul(pF[:, :], onesr_f[:, :], r_mr[:, :],
                              start=True, stop=True))
        bc_done = s_mm.v

        base3 = s_ev.v

        @block.vector
        def _(v, val=bc_done, base3=base3, ex=tuple(extra_wr)):
            v.wait_ge(s_mm.h, val)
            for xs_, xv_ in ex:
                if xv_:
                    v.wait_ge(xs_.h, xv_)
            v.tensor_copy(out=sbc[:, 0:TC], in_=pE[:, :])
            s_ev.inc(v.tensor_copy(out=sbc[:, TC:2 * TC], in_=pF[:, :]))
            v.wait_ge(s_ev.h, base3 + 1)
            for sig in range(8):
                mm = v.tensor_tensor(out=scrA[:, bass.ts(sig, TC)],
                                     in0=src_fm[:, bass.ts(sig, TC)],
                                     in1=sbc[:, 0:TC], op=ALU.mult)
            s_ev.inc(mm)
            v.wait_ge(s_ev.h, base3 + 2)
            for sig in range(8):
                mm = v.tensor_tensor(out=sln[:, bass.ts(sig, TC)],
                                     in0=scrA[:, bass.ts(sig, TC)],
                                     in1=sbc[:, TC:2 * TC], op=ALU.subtract)
            s_ev.inc(mm)
        bdone(pE, s_ev, s_ev.v)
        bdone(pF, s_ev, s_ev.v)
        return s_ev.v

    def emit_poll(s, flag_dyn_ap):
        rflag, rcond, riter = regs["rflag"], regs["rcond"], regs["riter"]
        s.reg_mov(rcond, 1)
        s.reg_mov(riter, 0)
        with s.While(rcond):
            s.dma_start(out=sfpoll[:, :], in_=flag_dyn_ap).then_inc(s_pl.h, 16)
            s.wait_ge(s_pl.h, 16)
            s.sem_clear(s_pl.h)
            s.reg_load(rflag, sfpoll[0:1, 0:1])
            s.reg_alu(riter, riter, 1, ALU.add)
            s.reg_alu(rcond, rflag, 1, ALU.is_lt)
            s.reg_alu(rflag, riter, POLL_CAP, ALU.is_lt)
            s.reg_alu(rcond, rcond, rflag, ALU.logical_and)

    x0tm_free = 0
    scan_prev = 0
    xcat_ok = [0, 0]
    scrA_guard = 0
    sln_free_A = 0
    prev_gate_ev = 0
    prev_bc_ld = 0
    prev_bc_mm = 0
    prev_dt_mm = 0

    for c in range(NCH):
        emit_cast_block(17)
        cxc = xcat[0]
        cxn = xcat[0]

        for tt in range(4):
            def _x0ld(s, bsel, par, c=c, tt=tt, fv=x0tm_free):
                if fv:
                    s.wait_ge(s_mm.h, fv)
                s.dma_start(
                    out=x0tm[:, :],
                    in_=x0_in.rearrange("b (ch tt p) d -> b ch tt p d", tt=4,
                                        p=128)[bsel, c, tt, :, :]
                ).then_inc(s_x0.h, 16)
            dsync(_x0ld)
            s_x0.v += 16
            tile_ld = s_x0.v

            @block.tensor
            def _(t, c=c, tt=tt, val=tile_ld):
                t.wait_ge(s_x0.h, val)
                for half in range(2):
                    pt = (pA, pB)[half]
                    bwait(t, pt)
                    for j in range(4):
                        sig = half * 4 + j
                        mm = t.transpose(
                            pt.ap().rearrange("p (j q) -> p j q", j=4)[
                                :, j, :],
                            x0tm[:, bass.ts(sig, 128)],
                            s_eye[:, :])
                    s_mm.inc(mm)
            tr_vals = [s_mm.v - 2 + i + 1 for i in range(2)]
            x0tm_free = tr_vals[-1]

            @block.vector
            def _(v, tt=tt, vals=tr_vals):
                for half in range(2):
                    pt = (pA, pB)[half]
                    v.wait_ge(s_mm.h, vals[half])
                    s_ev.inc(v.tensor_copy(
                        out=x0fm.ap().rearrange("p (sig q) -> p sig q",
                                                sig=8)[
                            :, half * 4:(half + 1) * 4,
                            tt * 128:(tt + 1) * 128],
                        in_=pt.ap().rearrange("p (j q) -> p j q", j=4)))
            bdone(pA, s_ev, s_ev.v - 1)
            bdone(pB, s_ev, s_ev.v)
        x0fm_ev = s_ev.v

        ln_ev = emit_ln(x0fm, s_ev, x0fm_ev, pRa_A, pRb_A,
                        extra_sq=[(s_xch, scrA_guard)],
                        extra_wr=[(s_xch, scrA_guard), (s_mm, sln_free_A)])

        xpre_base = s_ev.v

        @block.tensor
        def _(t, c=c, val=ln_ev, xok=xcat_ok[c % 2], xb=xpre_base):
            t.wait_ge(s_ev.h, val)
            if xok:
                t.wait_ge(s_sc.h, xok)
            for mu in range(8):
                pt = (pC, pD)[mu % 2]
                bwait(t, pt)
                if mu >= 2:
                    t.wait_ge(s_ev.h, xb + mu - 1)
                for sig in range(8):
                    mm = t.matmul(
                        pt[:, :],
                        sWinX.ap().rearrange("p (sig m q) -> p sig m q",
                                             sig=8, m=8)[:, sig, mu, :],
                        sln[:, bass.ts(sig, TC)],
                        start=(sig == 0), stop=(sig == 7))
                s_mm.inc(mm)
        winx_mm = [s_mm.v - 8 + i + 1 for i in range(8)]

        @block.vector
        def _(v, vals=winx_mm, cxc=cxc):
            for mu in range(8):
                pt = (pC, pD)[mu % 2]
                v.wait_ge(s_mm.h, vals[mu])
                s_ev.inc(v.tensor_copy(
                    out=cxc.ap().rearrange("p (mu w) -> p mu w", mu=8)[
                        :, mu, 3:515],
                    in_=pt[:, :]))
        xpre_evs = [s_ev.v - 8 + i + 1 for i in range(8)]
        bdone(pC, s_ev, xpre_evs[-2])
        bdone(pD, s_ev, xpre_evs[-1])

        zsil_base = s_sc.v

        @block.tensor
        def _(t, c=c, zb=zsil_base):
            for mu in range(8):
                pt = (pC, pD)[mu % 2]
                bwait(t, pt)
                if mu >= 2:
                    t.wait_ge(s_sc.h, zb + mu - 1)
                for sig in range(8):
                    mm = t.matmul(
                        pt[:, :],
                        sWinZ.ap().rearrange("p (sig m q) -> p sig m q",
                                             sig=8, m=8)[:, sig, mu, :],
                        sln[:, bass.ts(sig, TC)],
                        start=(sig == 0), stop=(sig == 7))
                s_mm.inc(mm)
        winz_mm = [s_mm.v - 8 + i + 1 for i in range(8)]

        @block.scalar
        def _(sc, vals=winz_mm, pg=prev_gate_ev):
            if pg:
                sc.wait_ge(s_ev.h, pg)
            for mu in range(8):
                pt = (pC, pD)[mu % 2]
                sc.wait_ge(s_mm.h, vals[mu])
                s_sc.inc(sc.activation(out=szs[:, bass.ts(mu, TC)],
                                       in_=pt[:, :], func=AF.Silu))
        z_scs = [s_sc.v - 8 + i + 1 for i in range(8)]
        bdone(pC, s_sc, z_scs[-2])
        bdone(pD, s_sc, z_scs[-1])

        conv_base_sc = s_sc.v

        conv_evs = []

        @block.vector
        def _(v, c=c, vals=xpre_evs, cxc=cxc, bsc=conv_base_sc):
            def F(instr):
                v.wait_ge(s_ev.h, s_ev.inc(instr))
            if c > 0:
                F(v.tensor_copy(
                    out=cxc.ap().rearrange("p (mu w) -> p mu w", mu=8)[
                        :, :, 0:3],
                    in_=ctx3.ap().rearrange("p (mu k) -> p mu k", mu=8)))
            for mu in range(8):
                v.wait_ge(s_ev.h, vals[mu])
                if mu >= 2:
                    v.wait_ge(s_sc.h, bsc + (mu - 2) + 1)
                ca, cb = (su_t[0], su_t[1]) if mu % 2 == 0 else (shh[0], shh[1])
                base = mu * 515
                F(v.tensor_scalar(out=ca[:, :], in0=cxc[:, base:base + TC],
                                  scalar1=sconvw[:, 4 * mu:4 * mu + 1],
                                  scalar2=None, op0=ALU.mult))
                F(v.scalar_tensor_tensor(
                    out=cb[:, :], in0=cxc[:, base + 1:base + 1 + TC],
                    scalar=sconvw[:, 4 * mu + 1:4 * mu + 2], in1=ca[:, :],
                    op0=ALU.mult, op1=ALU.add))
                F(v.scalar_tensor_tensor(
                    out=ca[:, :], in0=cxc[:, base + 2:base + 2 + TC],
                    scalar=sconvw[:, 4 * mu + 2:4 * mu + 3], in1=cb[:, :],
                    op0=ALU.mult, op1=ALU.add))
                F(v.scalar_tensor_tensor(
                    out=cb[:, :], in0=cxc[:, base + 3:base + 3 + TC],
                    scalar=sconvw[:, 4 * mu + 3:4 * mu + 4], in1=ca[:, :],
                    op0=ALU.mult, op1=ALU.add))
                s_ev.inc(v.tensor_copy(
                    out=ctx3[:, mu * 3:mu * 3 + 3],
                    in_=cxc[:, base + TC:base + TC + 3]))
                conv_evs.append(s_ev.v)

        @block.scalar
        def _(sc, vals=conv_evs, pg=prev_gate_ev):
            if pg:
                sc.wait_ge(s_ev.h, pg)
            for mu in range(8):
                buf = (su_t[1], shh[1])[mu % 2]
                sc.wait_ge(s_ev.h, vals[mu])
                s_sc.inc(sc.activation(out=sxs[:, bass.ts(mu, TC)],
                                       in_=buf[:, :], func=AF.Silu,
                                       bias=sconvb[:, mu:mu + 1]))
        xs_sc = s_sc.v
        xcat_ok[(c + 1) % 2] = xs_sc
        xcat_ok[c % 2] = xs_sc

        @block.tensor
        def _(t, val=xs_sc):
            t.wait_ge(s_sc.h, val)
            bwait(t, pE)
            for mu in range(8):
                mm = t.matmul(pE[0:96, :],
                              sWxT.ap().rearrange("p (mu j) -> p mu j",
                                                  mu=8)[:, mu, :],
                              sxs[:, bass.ts(mu, TC)],
                              start=(mu == 0), stop=(mu == 7))
            s_mm.inc(mm)
        wx_mm = s_mm.v

        @block.vector
        def _(v, val=wx_mm, pb=prev_bc_ld):
            v.wait_ge(s_mm.h, val)
            if pb:
                v.wait_ge(s_wld.h, pb)
            s_ev.inc(v.tensor_copy(out=sproj[:, :], in_=pE[0:96, :]))
        proj_ev = s_ev.v
        bdone(pE, s_ev, proj_ev)

        _pv1 = s_xch.v + 16

        def _projx(s, bsel, par, c=c, val=proj_ev, v1=_pv1):
            s.wait_ge(s_ev.h, val)
            s.dma_start(out=sh_proj[par, c, :, :],
                        in_=sproj[:, :]).then_inc(s_xch.h, 16)
            s.wait_ge(s_xch.h, v1)
            s.dma_start(out=sh_flag[par, c:c + 1, 0:1],
                        in_=sflag1[:, :]).then_inc(s_xch.h, 16)
            emit_poll(s, sh_flag[1 - par, c:c + 1, 0:1])
            s.dma_start(out=shh[0][0:96, :],
                        in_=sh_proj[1 - par, c, :, :]).then_inc(s_xch.h, 16)
        dsync(_projx)
        s_xch.v += 48
        projx_ld = s_xch.v

        @block.vector
        def _(v, val=projx_ld):
            v.wait_ge(s_xch.h, val)
            s_ev.inc(v.tensor_tensor(out=sproj[:, :], in0=sproj[:, :],
                                     in1=shh[0][0:96, :], op=ALU.add))
        projf_ev = s_ev.v

        @block.gpsimd
        def _(g, val=projf_ev, pb=prev_bc_mm):
            g.wait_ge(s_ev.h, val)
            if pb:
                g.wait_ge(s_mm.h, pb)
            s_wld.inc(g.dma_start(out=sBC32[:, :], in_=sproj[64:96, :]), 16)
        bc_ld = s_wld.v

        @block.vector
        def _(v, val=projf_ev, pd=prev_dt_mm):
            v.wait_ge(s_ev.h, val)
            if pd:
                v.wait_ge(s_mm.h, pd)
            s_ev.inc(v.tensor_copy(out=sBb[0:64, :], in_=sproj[0:64, :]))
        dtl_ev = s_ev.v

        dtsp_base = s_sc.v

        @block.tensor
        def _(t, val=dtl_ev, db=dtsp_base):
            t.wait_ge(s_ev.h, val)
            for mu in range(8):
                pt = (pC, pD)[mu % 2]
                bwait(t, pt)
                if mu >= 2:
                    t.wait_ge(s_sc.h, db + 3 * (mu - 2) + 1)
                s_mm.inc(t.matmul(
                    pt[:, :],
                    sWdt.ap().rearrange("p (mu q) -> p mu q", mu=8)[:, mu, :],
                    sBb[0:64, :], start=True, stop=True))
        dt_mms = [s_mm.v - 8 + i + 1 for i in range(8)]

        sp_base_sc = s_sc.v
        assert sp_base_sc == dtsp_base
        sp_base_ev = s_ev.v

        @block.scalar
        def _(sc, vals=dt_mms, bsc=sp_base_sc, bev=sp_base_ev,
              pg=prev_gate_ev):
            if pg:
                sc.wait_ge(s_ev.h, pg)
            for mu in range(8):
                pt = (pC, pD)[mu % 2]
                ta, tb = (sa_t[0], sa_t[1]) if mu % 2 == 0 else \
                    (su_t[0], su_t[1])
                sc.wait_ge(s_mm.h, vals[mu])
                if mu >= 2:
                    sc.wait_ge(s_ev.h, bev + (mu - 2) + 1)
                sc.activation(out=ta[:, :], in_=pt[:, :], func=AF.Abs,
                              bias=sbdt[:, mu:mu + 1])
                s_sc.inc(sc.activation(out=tb[:, :], in_=pt[:, :],
                                       func=AF.Relu, bias=sbdt[:, mu:mu + 1]))
                sc.wait_ge(s_sc.h, bsc + 3 * mu + 1)
                s_sc.inc(sc.activation(out=ta[:, :], in_=ta[:, :],
                                       func=AF.Exp, scale=-1.0))
                sc.wait_ge(s_sc.h, bsc + 3 * mu + 2)
                s_sc.inc(sc.activation(out=ta[:, :], in_=ta[:, :],
                                       func=AF.Ln, bias=1.0))
        dt_sps = [sp_base_sc + 3 * i + 3 for i in range(8)]
        assert s_sc.v == sp_base_sc + 24

        @block.vector
        def _(v, vals=dt_sps, bev=sp_base_ev):
            for mu in range(8):
                ta, tb = (sa_t[0], sa_t[1]) if mu % 2 == 0 else \
                    (su_t[0], su_t[1])
                v.wait_ge(s_sc.h, vals[mu])
                s_ev.inc(v.tensor_tensor(out=sdt[:, bass.ts(mu, TC)],
                                         in0=ta[:, :], in1=tb[:, :],
                                         op=ALU.add))
            v.wait_ge(s_ev.h, s_ev.v)
            s_ev.inc(v.tensor_tensor(out=sdtx[:, :], in0=sdt[:, :],
                                     in1=sxs[:, :], op=ALU.mult))
        dt_evs = [sp_base_ev + i + 1 for i in range(8)]
        dtx_ev = s_ev.v
        bdone(pC, s_ev, dt_evs[-2])
        bdone(pD, s_ev, dt_evs[-1])

        scan_base_ev = s_ev.v
        bc_evac_vals = []
        scan_op_vals = {}
        grp_vals = {}
        _v = scan_base_ev
        for si in range(NS):
            _v += 1
            bc_evac_vals.append(_v)
            for mu in range(8):
                _v += 1
                _v += 1
                scan_op_vals[(si, mu)] = _v
                if si > 0:
                    _v += 1
                _v += 1
                grp_vals[(si, mu)] = _v
        scan_end_ev = _v

        @block.tensor
        def _(t, val=bc_ld, bev=bc_evac_vals, selv=SEL_LD):
            t.wait_ge(s_wld.h, val)
            t.wait_ge(s_sel.h, selv)
            for si in range(NS):
                if si >= 1:
                    t.wait_ge(s_ev.h, bev[si - 1])
                bwait(t, pE)
                bwait(t, pF)
                t.matmul(pE[:, :], sel[:, bass.ts(si, 128)],
                         sBC32[:, :], start=True, stop=True)
                s_mm.inc(t.matmul(
                    pF[:, :], sel[:, bass.ts(NS + si, 128)],
                    sBC32[:, :], start=True, stop=True))
        bc_mms = [s_mm.v - NS + i + 1 for i in range(NS)]
        bdone(pE, s_ev, bc_evac_vals[-1])
        bdone(pF, s_ev, bc_evac_vals[-1])

        @block.scalar
        def _(sc, dtv=dt_evs, sov=dict(scan_op_vals)):
            for si in range(NS):
                for mu in range(8):
                    k = si * 8 + mu
                    buf = sa_t[k % 2]
                    if si == 0:
                        sc.wait_ge(s_ev.h, dtv[mu])
                    if k >= 2:
                        km2 = k - 2
                        sc.wait_ge(s_ev.h, sov[(km2 // 8, km2 % 8)])
                    s_sc.inc(sc.activation(
                        out=buf[:, :], in_=sdt[:, bass.ts(mu, TC)],
                        func=AF.Exp,
                        scale=sA[:, mu * 16 + si:mu * 16 + si + 1]))
        a_scs = [s_sc.v - NS * 8 + i + 1 for i in range(NS * 8)]

        @block.vector
        def _(v, bcv=bc_mms, asv=a_scs, dxv=dtx_ev, dtm=dt_mms[-1]):
            def F(instr):
                v.wait_ge(s_ev.h, s_ev.inc(instr))
            v.wait_ge(s_ev.h, dxv)
            v.wait_ge(s_mm.h, dtm)
            for si in range(NS):
                v.wait_ge(s_mm.h, bcv[si])
                v.tensor_copy(out=sBb[:, :], in_=pE[:, :])
                F(v.tensor_copy(out=sCb[:, :], in_=pF[:, :]))
                for mu in range(8):
                    k = si * 8 + mu
                    sabuf = sa_t[k % 2]
                    hbuf = shh[k % 2]
                    col = mu * 16 + si
                    v.wait_ge(s_sc.h, asv[k])
                    F(v.tensor_tensor(out=su_t[k % 2][:, :],
                                      in0=sdtx[:, bass.ts(mu, TC)],
                                      in1=sBb[:, :], op=ALU.mult))
                    F(v.tensor_tensor_scan(out=hbuf[:, :], data0=sabuf[:, :],
                                           data1=su_t[k % 2][:, :],
                                           initial=hstate[:, col:col + 1],
                                           op0=ALU.mult, op1=ALU.add))
                    v.tensor_copy(out=hstate[:, col:col + 1],
                                  in_=hbuf[:, TC - 1:TC])
                    if si == 0:
                        s_ev.inc(v.tensor_tensor(
                            out=scrA[:, bass.ts(mu, TC)], in0=hbuf[:, :],
                            in1=sCb[:, :], op=ALU.mult))
                    else:
                        F(v.tensor_tensor(out=syp[:, :], in0=hbuf[:, :],
                                          in1=sCb[:, :], op=ALU.mult))
                        s_ev.inc(v.tensor_tensor(
                            out=scrA[:, bass.ts(mu, TC)],
                            in0=scrA[:, bass.ts(mu, TC)], in1=syp[:, :],
                            op=ALU.add))
        assert s_ev.v == scan_end_ev, (s_ev.v, scan_end_ev)
        scan_ev = s_ev.v
        scan_prev = scan_ev

        @block.vector
        def _(v, zv=z_scs[-1], sev=scan_ev):
            def F(instr):
                v.wait_ge(s_ev.h, s_ev.inc(instr))
            v.wait_ge(s_sc.h, zv)
            v.wait_ge(s_ev.h, sev)
            for mu in range(8):
                F(v.scalar_tensor_tensor(out=syp[:, :],
                                         in0=sxs[:, bass.ts(mu, TC)],
                                         scalar=sDv[:, mu:mu + 1],
                                         in1=scrA[:, bass.ts(mu, TC)],
                                         op0=ALU.mult, op1=ALU.add))
                F(v.tensor_tensor(out=sdtx[:, bass.ts(mu, TC)],
                                  in0=syp[:, :],
                                  in1=szs[:, bass.ts(mu, TC)],
                                  op=ALU.mult))
        gate_ev = s_ev.v

        wo_base = s_ev.v

        @block.tensor
        def _(t, val=gate_ev, wb=wo_base):
            t.wait_ge(s_ev.h, val)
            for m in range(8):
                pt = (pC, pD)[m % 2]
                bwait(t, pt)
                if m >= 2:
                    t.wait_ge(s_ev.h, wb + m - 1)
                for mu in range(8):
                    mm = t.matmul(
                        pt[:, :],
                        sWout.ap().rearrange("p (mu m q) -> p mu m q",
                                             mu=8, m=8)[:, mu, m, :],
                        sdtx[:, bass.ts(mu, TC)],
                        start=(mu == 0), stop=(mu == 7))
                s_mm.inc(mm)
        wo_mms = [s_mm.v - 8 + i + 1 for i in range(8)]

        @block.vector
        def _(v, vals=wo_mms):
            for m in range(8):
                pt = (pC, pD)[m % 2]
                v.wait_ge(s_mm.h, vals[m])
                s_ev.inc(v.scalar_tensor_tensor(
                    out=scrA[:, bass.ts(m, TC)], in0=x0fm[:, bass.ts(m, TC)],
                    scalar=0.5, in1=pt[:, :], op0=ALU.mult, op1=ALU.add))
        po_evs = [s_ev.v - 8 + i + 1 for i in range(8)]
        pout_ev = s_ev.v
        bdone(pC, s_ev, po_evs[-2])
        bdone(pD, s_ev, po_evs[-1])

        _pv2 = s_xch.v + 16

        def _poutw(s, bsel, par, c=c, val=pout_ev, v1=_pv2):
            s.wait_ge(s_ev.h, val)
            s.dma_start(
                out=sh_pout.rearrange("a c (m p) q -> a c p m q", m=8)[
                    par, c, :, :, :],
                in_=scrA.ap().rearrange("p (m q) -> p m q", m=8)
            ).then_inc(s_xch.h, 16)
            s.wait_ge(s_xch.h, v1)
            s.dma_start(out=sh_flag[par, 8 + c:9 + c, 0:1],
                        in_=sflag1[:, :]).then_inc(s_xch.h, 16)
        dsync(_poutw)
        s_xch.v += 32
        scrA_guard = s_xch.v
        sln_free_A = winz_mm[-1]
        prev_gate_ev = gate_ev
        prev_bc_ld = bc_ld
        prev_bc_mm = bc_mms[-1]
        prev_dt_mm = dt_mms[-1]

    flush_sync()
    barrier()
    if PHASE_LIMIT <= 2:
        return debug_finish()
    scA.close()

    scHA = Scope()
    hA = scHA.SB("hA", [128, 16384], BF16)
    scX = Scope()
    X = scX.SB("X", [128, 16384], F32)
    hB = scX.SB("hB", [128, 16384], BF16)
    scB1 = Scope()
    wbuf = [scB1.SB(f"wbuf{i}", [128, 4096], BF16) for i in range(2)]
    spo = scB1.SB("spo", [128, 4096], F32)
    fst1 = scB1.SB("fst1", [128, 512], F32)
    scB1b = Scope()
    pR1 = scB1b.PSUM("pR1", [1, 512])
    pR2 = scB1b.PSUM("pR2", [1, 512])

    wbuf_free = [0, 0]

    def emit_ffn(xoff, ln_ready, wbuf, fst1, hA, hB, X):
        nonlocal wbuf_free
        last_w2_mm = 0
        g1_scs = []
        for m in range(32):
            wb = wbuf[m % 2]
            ws = (s_wA, s_wB)[m % 2]

            @block.gpsimd
            def _(g, m=m, wb=wb, ws=ws, pu=wbuf_free[m % 2]):
                if pu:
                    g.wait_ge(s_mm.h, pu)
                ws.inc(g.dma_start(out=wb.ap()[:, 0:1024],
                                   in_=dw1c[m, :, :]), 16)
            wld = ws.v

            @block.tensor
            def _(t, m=m, wb=wb, ws=ws, val=wld, lnv=ln_ready):
                t.wait_ge(ws.h, val)
                if m == 0:
                    t.wait_ge(s_ev.h, lnv)
                pt = (pC, pD)[m % 2]
                bwait(t, pt)
                for sig in range(8):
                    mm = t.matmul(
                        pt[:, :],
                        wb.ap()[:, 0:1024].rearrange("p (sig q) -> p sig q",
                                                     sig=8)[:, sig, :],
                        sln[:, bass.ts(sig, TC)],
                        start=(sig == 0), stop=(sig == 7))
                s_mm.inc(mm)
            mmv = s_mm.v
            wbuf_free[m % 2] = mmv

            @block.scalar
            def _(sc, m=m, val=mmv):
                pt = (pC, pD)[m % 2]
                sc.wait_ge(s_mm.h, val)
                s_sc.inc(sc.activation(out=hB[:, bass.ts(m, TC)],
                                       in_=pt[:, :], func=AF.Silu,
                                       scale=sw1s[:, m:m + 1]))
            g1_scs.append(s_sc.v)
            bdone((pC, pD)[m % 2], s_sc, s_sc.v)
        g1_done = s_sc.v

        for m in range(32):
            wb = wbuf[m % 2]
            ws = (s_wA, s_wB)[m % 2]

            @block.gpsimd
            def _(g, m=m, wb=wb, ws=ws, pu=wbuf_free[m % 2]):
                if pu:
                    g.wait_ge(s_mm.h, pu)
                ws.inc(g.dma_start(out=wb.ap()[:, 0:1024],
                                   in_=dw2c[m, :, :]), 16)
            wld = ws.v

            @block.tensor
            def _(t, m=m, wb=wb, ws=ws, val=wld):
                t.wait_ge(ws.h, val)
                pt = (pC, pD)[m % 2]
                bwait(t, pt)
                for sig in range(8):
                    mm = t.matmul(
                        pt[:, :],
                        wb.ap()[:, 0:1024].rearrange("p (sig q) -> p sig q",
                                                     sig=8)[:, sig, :],
                        sln[:, bass.ts(sig, TC)],
                        start=(sig == 0), stop=(sig == 7))
                s_mm.inc(mm)
            mmv = s_mm.v
            wbuf_free[m % 2] = mmv
            last_w2_mm = mmv

            @block.vector
            def _(v, m=m, val=mmv, gv=g1_scs[m]):
                pt = (pC, pD)[m % 2]
                v.wait_ge(s_mm.h, val)
                v.wait_ge(s_sc.h, gv)
                s_ev.inc(v.scalar_tensor_tensor(
                    out=hA[:, bass.ts(m, TC)], in0=pt[:, :],
                    scalar=sw2s[:, m:m + 1], in1=hB[:, bass.ts(m, TC)],
                    op0=ALU.mult, op1=ALU.mult))
            bdone((pC, pD)[m % 2], s_ev, s_ev.v)
        g_done = s_ev.v

        for stage, (hT, scl_t, src, dst) in enumerate(
                [(dh1c, sh1s, hA, hB), (dh2c, sh2s, hB, hA)]):
            src_sem = s_ev if stage == 0 else s_sc
            src_val = g_done if stage == 0 else s_sc.v
            for m in range(32):
                wb = wbuf[m % 2]
                ws = (s_wA, s_wB)[m % 2]

                @block.gpsimd
                def _(g, m=m, hT=hT, wb=wb, ws=ws, pu=wbuf_free[m % 2]):
                    if pu:
                        g.wait_ge(s_mm.h, pu)
                    ws.inc(g.dma_start(out=wb.ap(), in_=hT[m, :, :]), 16)
                wld = ws.v

                @block.tensor
                def _(t, m=m, wb=wb, ws=ws, val=wld, src=src, ssem=src_sem,
                      sval=src_val):
                    t.wait_ge(ws.h, val)
                    if m == 0:
                        t.wait_ge(ssem.h, sval)
                    pt = (pC, pD)[m % 2]
                    bwait(t, pt)
                    for sig in range(32):
                        mm = t.matmul(
                            pt[:, :],
                            wb.ap().rearrange("p (sig q) -> p sig q",
                                              sig=32)[:, sig, :],
                            src[:, sig * 128:(sig + 1) * 128] if False else
                            src[:, bass.ts(sig, TC)][:, 0:TC],
                            start=(sig == 0), stop=(sig == 31))
                    s_mm.inc(mm)
                mmv = s_mm.v
                wbuf_free[m % 2] = mmv

                @block.scalar
                def _(sc, m=m, val=mmv, scl_t=scl_t, dst=dst):
                    pt = (pC, pD)[m % 2]
                    sc.wait_ge(s_mm.h, val)
                    s_sc.inc(sc.activation(out=dst[:, bass.ts(m, TC)],
                                           in_=pt[:, :], func=AF.Gelu,
                                           scale=scl_t[:, m:m + 1]))
                bdone((pC, pD)[m % 2], s_sc, s_sc.v)
        h_done = s_sc.v

        res_evs = []
        for m in range(8):
            wb = wbuf[m % 2]
            ws = (s_wA, s_wB)[m % 2]

            @block.gpsimd
            def _(g, m=m, wb=wb, ws=ws, pu=wbuf_free[m % 2]):
                if pu:
                    g.wait_ge(s_mm.h, pu)
                ws.inc(g.dma_start(out=wb.ap(), in_=dw3c[m, :, :]), 16)
            wld = ws.v

            @block.tensor
            def _(t, m=m, wb=wb, ws=ws, val=wld, hd=h_done):
                t.wait_ge(ws.h, val)
                if m == 0:
                    t.wait_ge(s_sc.h, hd)
                pt = (pC, pD)[m % 2]
                bwait(t, pt)
                for sig in range(32):
                    mm = t.matmul(
                        pt[:, :],
                        wb.ap().rearrange("p (sig q) -> p sig q", sig=32)[
                            :, sig, :],
                        hA[:, bass.ts(sig, TC)][:, 0:TC],
                        start=(sig == 0), stop=(sig == 31))
                s_mm.inc(mm)
            mmv = s_mm.v
            wbuf_free[m % 2] = mmv

            base = s_ev.v

            @block.vector
            def _(v, m=m, val=mmv, xoff=xoff, base=base):
                pt = (pC, pD)[m % 2]
                v.wait_ge(s_mm.h, val)
                s_ev.inc(v.tensor_scalar(out=fst1[:, :], in0=pt[:, :],
                                         scalar1=sw3s[:, m:m + 1],
                                         scalar2=None, op0=ALU.mult))
                v.wait_ge(s_ev.h, base + 1)
                s_ev.inc(v.tensor_tensor(
                    out=X[:, xoff + m * TC:xoff + (m + 1) * TC],
                    in0=X[:, xoff + m * TC:xoff + (m + 1) * TC],
                    in1=fst1[:, :], op=ALU.add))
            res_evs.append(s_ev.v)
            bdone((pC, pD)[m % 2], s_ev, s_ev.v)
        return s_ev.v, last_w2_mm

    sln_free_B = 0
    spo_free_B = 0
    for sl in range(4):
        def _x1ld(s, bsel, par, sl=sl, spf=spo_free_B):
            if spf:
                s.wait_ge(s_ev.h, spf)
            g = 4 * par + sl
            emit_poll(s, sh_flag[1 - par, 8 + g:9 + g, 0:1])
            s.dma_start(
                out=X.ap()[:, bass.ts(sl, 4096)].rearrange(
                    "p (m q) -> p m q", m=8),
                in_=sh_pout.rearrange("a c (m p) q -> a c p m q", m=8)[
                    par, g, :, :, :]).then_inc(s_xld.h, 16)
            s.dma_start(
                out=spo.ap().rearrange("p (m q) -> p m q", m=8),
                in_=sh_pout.rearrange("a c (m p) q -> a c p m q", m=8)[
                    1 - par, g, :, :, :]).then_inc(s_xld.h, 16)
        dsync(_x1ld)
        s_xld.v += 32
        x1_ld = s_xld.v

        @block.vector
        def _(v, sl=sl, val=x1_ld):
            v.wait_ge(s_xld.h, val)
            s_ev.inc(v.tensor_tensor(out=X[:, bass.ts(sl, 4096)],
                                     in0=X[:, bass.ts(sl, 4096)],
                                     in1=spo[:, :], op=ALU.add))
        x1_ev = s_ev.v
        spo_free_B = x1_ev

        lnv = emit_ln(X.ap()[:, bass.ts(sl, 4096)], s_ev, x1_ev, pR1, pR2,
                      extra_wr=[(s_mm, sln_free_B)])
        _, sln_free_B = emit_ffn(sl * 4096, lnv, wbuf, fst1, hA, hB, X)

    flush_sync()
    barrier()
    if PHASE_LIMIT <= 3:
        return debug_finish()
    scB1.close()

    scW = Scope()
    swT = scW.SB("swT", [128, 8192], BF16, side="right")
    scCp = Scope()
    sstg = scCp.SB("sstg", [128, 2048], BF16)

    @block.gpsimd
    def _(g):
        s_wld.inc(g.dma_start(
            out=swT.ap().rearrange("p (sig m) -> p sig m", sig=8),
            in_=W_spinT.rearrange("(sig p) m -> p sig m", sig=8)), 16)
    wspin_ld = s_wld.v

    for sl in range(4):
        lnv = emit_ln(X.ap()[:, bass.ts(sl, 4096)], s_ev, s_ev.v, pR1, pR2)

        @block.vector
        def _(v, sl=sl, val=lnv):
            v.wait_ge(s_ev.h, val)
            s_ev.inc(v.tensor_copy(out=hB[:, bass.ts(sl, 4096)],
                                   in_=sln[:, :]))
    ln2_all = s_ev.v

    @block.sync
    def _(s, val=ln2_all):
        s.wait_ge(s_ev.h, val)
        s_xch.inc(s.dma_start(out=dr_x2[:, :], in_=X[:, :]), 16)
    x2_dumped = s_xch.v

    sstg_guard = 0
    for m in range(8):
        sev_base = s_ev.v

        @block.tensor
        def _(t, m=m, val=wspin_ld, lv=ln2_all, sb=sev_base):
            t.wait_ge(s_wld.h, val)
            if m == 0:
                t.wait_ge(s_ev.h, lv)
            for nt in range(4):
                pt = (pC, pD)[nt % 2]
                bwait(t, pt)
                if nt >= 2:
                    t.wait_ge(s_ev.h, sb + nt - 1)
                for sig in range(8):
                    mm = t.matmul(
                        pt[:, :],
                        swT.ap().rearrange("p (sig m) -> p sig m", sig=8)[
                            :, sig, bass.ts(m, 128)],
                        hB.ap().rearrange("p (sl sig q) -> p sl sig q",
                                          sl=4, sig=8)[:, nt, sig, :],
                        start=(sig == 0), stop=(sig == 7))
                s_mm.inc(mm)
        s_mms = [s_mm.v - 4 + i + 1 for i in range(4)]

        @block.vector
        def _(v, m=m, vals=s_mms, gv=sstg_guard):
            if gv:
                v.wait_ge(s_xch.h, gv)
            for nt in range(4):
                pt = (pC, pD)[nt % 2]
                v.wait_ge(s_mm.h, vals[nt])
                s_ev.inc(v.tensor_scalar(out=sstg[:, bass.ts(nt, TC)],
                                         in0=pt[:, :],
                                         scalar1=sbspin[:, m:m + 1],
                                         scalar2=None, op0=ALU.add))
            v.wait_ge(s_ev.h, s_ev.v)
            s_ev.inc(v.reduce_sum(smean[:, m:m + 1], sstg[:, :], axis=AX.X))
        scv = [s_ev.v - 5 + i + 1 for i in range(4)]
        bdone(pC, s_ev, scv[-2])
        bdone(pD, s_ev, scv[-1])

        def _sstore(s, bsel, par, m=m, val=scv[-1]):
            s.wait_ge(s_ev.h, val)
            s.dma_start(out=sh_s[par, m * 128:(m + 1) * 128, :],
                        in_=sstg[:, :]).then_inc(s_xch.h, 16)
        dsync(_sstore)
        s_xch.v += 16
        st_ld = s_xch.v
        sstg_guard = st_ld
    s_written = s_xch.v
    own_mean_ev = s_ev.v

    def _sflag(s, bsel, par, val=s_written):
        s.wait_ge(s_xch.h, val)
        s.dma_start(out=sh_flag[par, 16:17, 0:1],
                    in_=sflag1[:, :]).then_inc(s_xch.h, 16)
    dsync(_sflag)
    s_xch.v += 16
    s_flagged = s_xch.v

    _mv1 = s_xch.v + 16

    def _meanx(s, bsel, par, val=own_mean_ev, v1=_mv1):
        s.wait_ge(s_ev.h, val)
        s.dma_start(out=sh_mean[par, :, :],
                    in_=smean[:, :]).then_inc(s_xch.h, 16)
        s.wait_ge(s_xch.h, v1)
        s.dma_start(out=sh_flag[par, 17:18, 0:1],
                    in_=sflag1[:, :]).then_inc(s_xch.h, 16)
        emit_poll(s, sh_flag[1 - par, 17:18, 0:1])
        s.dma_start(out=smean_p[:, :],
                    in_=sh_mean[1 - par, :, :]).then_inc(s_xch.h, 16)
    dsync(_meanx)
    s_xch.v += 48
    meanx_ld = s_xch.v

    mb_base = s_ev.v

    @block.vector
    def _(v, val=meanx_ld, base=mb_base, omv=own_mean_ev):
        v.wait_ge(s_xch.h, val)
        v.wait_ge(s_ev.h, omv)
        s_ev.inc(v.tensor_tensor(out=smean[:, :], in0=smean[:, :],
                                 in1=smean_p[:, :], op=ALU.add))
        v.wait_ge(s_ev.h, base + 1)
        s_ev.inc(v.tensor_scalar(out=smean_b[:, :], in0=smean[:, :],
                                 scalar1=1.0 / 4096.0, scalar2=None,
                                 op0=ALU.mult))
    mean_ev = s_ev.v

    @block.gpsimd
    def _(g, val=s_mm.v):
        g.wait_ge(s_mm.h, val)
        s_wld.inc(g.dma_start(
            out=swT.ap().rearrange("p (sig m) -> p sig m", sig=8),
            in_=W_JT.rearrange("(sig p) m -> p sig m", sig=8)), 16)
    wj_ld = s_wld.v

    flush_sync()
    barrier()
    if PHASE_LIMIT <= 4:
        return debug_finish()
    scCp.close()
    scX.close()
    scB1b.close()

    scC = Scope()
    sfm = scC.SB("sfm", [128, 32768], BF16)
    sTb = [scC.SB(f"sTb{i}", [128, 512], BF16) for i in range(2)]
    sprobs = scC.SB("sprobs", [128, 4096], BF16)
    sprobT = scC.SB("sprobT", [128, 16384], BF16)
    sloc = scC.SB("sloc", [128, 2048], F32)
    ep1 = scC.SB("ep1", [128, 512], F32)
    smaxv = scC.SB("smaxv", [128, 1], F32)
    snmax = scC.SB("snmax", [128, 1], F32)
    ssum = scC.SB("ssum", [128, 1], F32)
    srec = scC.SB("srec", [128, 1], F32)
    pTb = scC.PSUM("pTb", [128, 1024], BF16)

    def _sfmld(s, bsel, par, val=s_flagged):
        s.wait_ge(s_xch.h, val)
        emit_poll(s, sh_flag[1 - par, 16:17, 0:1])
        s.dma_start(
            out=sfm.ap().rearrange("p (sig m) -> p sig m", sig=8)[
                :, :, 0:2048],
            in_=sh_s.rearrange("a (sig p) m -> a p sig m", sig=8)[
                par, :, :, :]).then_inc(s_xld.h, 16)
        s.dma_start(
            out=sfm.ap().rearrange("p (sig m) -> p sig m", sig=8)[
                :, :, 2048:4096],
            in_=sh_s.rearrange("a (sig p) m -> a p sig m", sig=8)[
                1 - par, :, :, :]).then_inc(s_xld.h, 16)
    dsync(_sfmld)
    s_xld.v += 32
    sfm_ld = s_xld.v

    for m in range(8):
        eev_base = s_ev.v

        @block.tensor
        def _(t, m=m, val=wj_ld, sfv=sfm_ld, eb=eev_base):
            t.wait_ge(s_wld.h, val)
            if m == 0:
                t.wait_ge(s_xld.h, sfv)
            for nt in range(4):
                pt = (pC, pD)[nt % 2]
                bwait(t, pt)
                if nt >= 2:
                    t.wait_ge(s_ev.h, eb + nt - 1)
                for sig in range(8):
                    mm = t.matmul(
                        pt[:, :],
                        swT.ap().rearrange("p (sig m) -> p sig m", sig=8)[
                            :, sig, bass.ts(m, 128)],
                        sfm.ap().rearrange("p (sig k) -> p sig k", sig=8)[
                            :, sig, nt * TC:(nt + 1) * TC],
                        start=(sig == 0), stop=(sig == 7))
                s_mm.inc(mm)
        e_mms = [s_mm.v - 4 + i + 1 for i in range(4)]

        @block.vector
        def _(v, m=m, vals=e_mms):
            for nt in range(4):
                pt = (pC, pD)[nt % 2]
                v.wait_ge(s_mm.h, vals[nt])
                s_ev.inc(v.tensor_copy(
                    out=hA[:, m * 2048 + nt * TC:m * 2048 + (nt + 1) * TC],
                    in_=pt[:, :]))
        e_evs = [s_ev.v - 4 + i + 1 for i in range(4)]
        bdone(pC, s_ev, e_evs[-2])
        bdone(pD, s_ev, e_evs[-1])
    energy_ev = s_ev.v

    @block.gpsimd
    def _(g, val=s_mm.v):
        g.wait_ge(s_mm.h, val)
        s_wld.inc(g.dma_start(
            out=swT.ap().rearrange("p (sig m) -> p sig m", sig=8),
            in_=W_globT.rearrange("(sig p) m -> p sig m", sig=8)), 16)
    wg_ld = s_wld.v

    @block.tensor
    def _(t, val=wg_ld, mv=mean_ev):
        t.wait_ge(s_wld.h, val)
        t.wait_ge(s_ev.h, mv)
        bwait(t, pE)
        for m in range(8):
            for sig in range(8):
                mm = t.matmul(
                    pE[:, m:m + 1],
                    swT.ap().rearrange("p (sig m) -> p sig m", sig=8)[
                        :, sig, bass.ts(m, 128)],
                    smean_b[:, sig:sig + 1],
                    start=(sig == 0), stop=(sig == 7), skip_group_check=True)
        s_mm.inc(mm)
    glob_mm = s_mm.v

    @block.vector
    def _(v, val=glob_mm):
        v.wait_ge(s_mm.h, val)
        s_ev.inc(v.tensor_copy(out=sglob[:, :], in_=pE[:, 0:8]))
    glob_ev = s_ev.v
    bdone(pE, s_ev, glob_ev)
    scW.close()

    sT_free = [0, 0]
    prev_prob_sc = 0
    prev_pv_mm = 0
    for ng in range(4):
        for qt in range(4):
            nq0 = ng * 512 + qt * 128

            scb = s_ev.v

            @block.tensor
            def _(t, ng=ng, qt=qt, nq0=nq0, ev=energy_ev, sv=sfm_ld,
                  scb=scb):
                if ng == 0 and qt == 0:
                    t.wait_ge(s_ev.h, ev)
                    t.wait_ge(s_xld.h, sv)
                for mt in range(8):
                    pt = (pC, pD)[mt % 2]
                    bwait(t, pt)
                    if mt >= 2:
                        t.wait_ge(s_ev.h, scb + mt - 1)
                    for sig in range(8):
                        mm = t.matmul(
                            pt[:, :],
                            hA.ap().rearrange("p (sig n) -> p sig n",
                                              sig=8)[:, sig, nq0:nq0 + 128],
                            sfm.ap().rearrange("p (sig m) -> p sig m",
                                               sig=8)[:, sig, bass.ts(mt, TC)],
                            start=(sig == 0), stop=(sig == 7))
                    s_mm.inc(mm)
            sc_mms = [s_mm.v - 8 + i + 1 for i in range(8)]

            @block.vector
            def _(v, vals=sc_mms, pps=prev_prob_sc):
                if pps:
                    v.wait_ge(s_sc.h, pps)
                for mt in range(8):
                    pt = (pC, pD)[mt % 2]
                    v.wait_ge(s_mm.h, vals[mt])
                    s_ev.inc(v.tensor_copy(out=scrA[:, bass.ts(mt, TC)],
                                           in_=pt[:, :]))
            scev = [s_ev.v - 8 + i + 1 for i in range(8)]
            bdone(pC, s_ev, scev[-2])
            bdone(pD, s_ev, scev[-1])

            if DEBUG_ATTN and ng == 0 and qt == 0:
                def _dbg1(s, bsel, par, val=scev[-1]):
                    s.wait_ge(s_ev.h, val)
                    s.dma_start(out=dr_dbg[:, 0:4096],
                                in_=scrA[:, :]).then_inc(s_ob.h, 16)
                    s.dma_start(out=dr_dbg[:, 8192:12288],
                                in_=scrA[:, :]).then_inc(s_ob.h, 16)
                    s.dma_start(out=dr_dbg[:, 12288:16384],
                                in_=scrA[:, :]).then_inc(s_ob.h, 16)
                dsync(_dbg1)
                s_ob.v += 48
                DBG1 = s_ob.v

                @block.vector
                def _(v, val=DBG1):
                    v.wait_ge(s_ob.h, val)

            mxb = s_ev.v

            @block.vector
            def _(v, base=mxb):
                v.wait_ge(s_ev.h, base)
                s_ev.inc(v.reduce_max(smaxv[:, :], scrA[:, :], axis=AX.X))
                v.wait_ge(s_ev.h, base + 1)
                s_ev.inc(v.tensor_scalar(out=snmax[:, :], in0=smaxv[:, :],
                                         scalar1=-1.0 / 32.0, scalar2=None,
                                         op0=ALU.mult))
            nmax_ev = s_ev.v

            @block.scalar
            def _(sc, val=nmax_ev):
                sc.wait_ge(s_ev.h, val)
                s_sc.inc(sc.activation(out=sprobs[:, :], in_=scrA[:, :],
                                       func=AF.Exp, scale=1.0 / 32.0,
                                       bias=snmax[:, 0:1],
                                       accum_out=ssum[:, 0:1]))
            prob_sc = s_sc.v
            prev_prob_sc = prob_sc

            rec_base = s_ev.v

            @block.vector
            def _(v, val=prob_sc, base=rec_base):
                v.wait_ge(s_sc.h, val)
                s_ev.inc(v.reciprocal(srec[:, :], ssum[:, :]))
                v.wait_ge(s_ev.h, base + 1)
                s_ev.inc(v.tensor_scalar(out=sprobs[:, :], in0=sprobs[:, :],
                                         scalar1=srec[:, 0:1], scalar2=None,
                                         op0=ALU.mult))
            probs_ev = s_ev.v

            if DEBUG_ATTN and ng == 0 and qt == 0:
                @block.gpsimd
                def _(g, val=probs_ev):
                    g.wait_ge(s_ev.h, val)
                    s_wld.inc(g.dma_start(out=dr_dbg[:, 4096:8192],
                                          in_=sprobs[:, :]), 16)
                DBG2 = s_wld.v

                @block.tensor
                def _(t, val=DBG2):
                    t.wait_ge(s_wld.h, val)

            for grp in range(4):
                @block.tensor
                def _(t, grp=grp, val=probs_ev):
                    if grp == 0:
                        t.wait_ge(s_ev.h, val)
                    bwait(t, pTb)
                    for j in range(8):
                        mj = grp * 8 + j
                        mm = t.transpose(
                            pTb.ap().rearrange("p (j q) -> p j q", j=8)[
                                :, j, :],
                            sprobs[:, mj * 128:(mj + 1) * 128],
                            s_eye_b[:, :])
                    s_mm.inc(mm)
                tr_mm = s_mm.v

                @block.vector
                def _(v, grp=grp, qt=qt, val=tr_mm, ppv=prev_pv_mm):
                    if ppv:
                        v.wait_ge(s_mm.h, ppv)
                    v.wait_ge(s_mm.h, val)
                    s_ev.inc(v.tensor_copy(
                        out=sprobT.ap().rearrange("p (mj q) -> p mj q",
                                                  mj=32)[
                            :, grp * 8:(grp + 1) * 8,
                            qt * 128:(qt + 1) * 128],
                        in_=pTb.ap().rearrange("p (j q) -> p j q", j=8)))
                bdone(pTb, s_ev, s_ev.v)
        probT_ev = s_ev.v

        for dg in range(2):
            for mj in range(32):
                tb = sTb[mj % 2]

                ts_sem = (s_tA, s_tB)[mj % 2]

                def _stld(s, bsel, par, dg=dg, mj=mj, tb=tb, ts_sem=ts_sem,
                          pu=sT_free[mj % 2]):
                    if pu:
                        s.wait_ge(s_mm.h, pu)
                    h = par if mj < 16 else 1 - par
                    lm = (mj % 16) * 128
                    s.dma_start_transpose(
                        tb[:, 0:512],
                        sh_s[h, dg * 512:(dg + 1) * 512,
                             lm:lm + 128]).then_inc(ts_sem.h, 16)
                dsync(_stld)
                ts_sem.v += 16
                tld = ts_sem.v

                @block.tensor
                def _(t, dg=dg, mj=mj, tb=tb, val=tld, pv=probT_ev,
                      ts_sem=ts_sem):
                    t.wait_ge(ts_sem.h, val)
                    if mj == 0:
                        t.wait_ge(s_ev.h, pv)
                    for dd in range(4):
                        pt = (pA, pB, pC, pD)[dd]
                        if mj == 0:
                            bwait(t, pt)
                        mm = t.matmul(
                            pt[:, :],
                            tb[:, dd * 128:(dd + 1) * 128],
                            sprobT.ap().rearrange("p (mj q) -> p mj q",
                                                  mj=32)[:, mj, :],
                            start=(mj == 0), stop=(mj == 31),
                            skip_group_check=True)
                    s_mm.inc(mm)
                sT_free[mj % 2] = s_mm.v
            pv_mm = s_mm.v
            prev_pv_mm = pv_mm

            ep_base = s_ev.v

            @block.vector
            def _(v, ng=ng, dg=dg, val=pv_mm, gv=glob_ev, base=ep_base):
                v.wait_ge(s_mm.h, val)
                v.wait_ge(s_ev.h, gv)
                for dd in range(4):
                    d = dg * 4 + dd
                    pt = (pA, pB, pC, pD)[dd]
                    s_ev.inc(v.scalar_tensor_tensor(
                        out=ep1[:, :], in0=pt[:, :], scalar=1.0,
                        in1=sfm.ap().rearrange("p (sig m) -> p sig m",
                                               sig=8)[
                            :, d, ng * 512:(ng + 1) * 512],
                        op0=ALU.mult, op1=ALU.add))
                    v.wait_ge(s_ev.h, base + 2 * dd + 1)
                    s_ev.inc(v.tensor_scalar(
                        out=sloc[:, bass.ts(dd, TC)], in0=ep1[:, :],
                        scalar1=sglob[:, d:d + 1], scalar2=sbeta[:, 0:1],
                        op0=ALU.add, op1=ALU.mult))
            ep_ev = s_ev.v
            for dd in range(4):
                bdone((pA, pB, pC, pD)[dd], s_ev, ep_ev)

            @block.scalar
            def _(sc, val=ep_ev):
                sc.wait_ge(s_ev.h, val)
                s_sc.inc(sc.activation(out=sloc[:, :], in_=sloc[:, :],
                                       func=AF.Tanh))
            tanh_sc = s_sc.v

            def _obst(s, bsel, par, ng=ng, dg=dg, val=tanh_sc):
                s.wait_ge(s_sc.h, val)
                s.dma_start(
                    out=dr_ob.rearrange("p (sl sig q) -> p sl sig q",
                                        sl=4, sig=8)[
                        :, ng, dg * 4:(dg + 1) * 4, :],
                    in_=sloc.ap().rearrange("p (dd q) -> p dd q", dd=4)
                ).then_inc(s_ob.h, 16)
            dsync(_obst)
            s_ob.v += 16
            sloc_guard = s_ob.v

            @block.vector
            def _(v, val=sloc_guard):
                v.wait_ge(s_ob.h, val)

    flush_sync()
    barrier()
    if PHASE_LIMIT <= 5:
        return debug_finish()
    scC.close()
    scHA.close()

    scD = Scope()
    X = scD.SB("X2", [128, 16384], F32)
    hA = scD.SB("hA2", [128, 16384], BF16)
    hB = scD.SB("hB2", [128, 16384], BF16)
    wbuf = [scD.SB(f"wbuf2{i}", [128, 4096], BF16) for i in range(2)]
    spo = scD.SB("spo2", [128, 4096], F32)
    fst1 = scD.SB("fst12", [128, 512], F32)
    pR1 = scD.PSUM("pR1d", [1, 512])
    pR2 = scD.PSUM("pR2d", [1, 512])
    souttm = spo
    wbuf_free = [0, 0]

    sln_free_D = 0
    spo_free_D = 0
    prev_store = 0
    for sl in range(4):
        @block.sync
        def _(s, sl=sl, spf=spo_free_D, pst=prev_store):
            if spf:
                s.wait_ge(s_ev.h, spf)
            if pst:
                s.wait_ge(s_st.h, pst)
            s_xld.inc(s.dma_start(out=X[:, bass.ts(sl, 4096)],
                                  in_=dr_x2[:, bass.ts(sl, 4096)]), 16)
            s_xld.inc(s.dma_start(out=spo[:, :],
                                  in_=dr_ob[:, bass.ts(sl, 4096)]), 16)
        x3_ld = s_xld.v

        @block.vector
        def _(v, sl=sl, val=x3_ld):
            v.wait_ge(s_xld.h, val)
            s_ev.inc(v.tensor_tensor(out=X[:, bass.ts(sl, 4096)],
                                     in0=X[:, bass.ts(sl, 4096)],
                                     in1=spo[:, :], op=ALU.add))
        x3_ev = s_ev.v
        spo_free_D = x3_ev

        lnv = emit_ln(X.ap()[:, bass.ts(sl, 4096)], s_ev, x3_ev, pR1, pR2,
                      extra_wr=[(s_mm, sln_free_D)])
        ffn_done, sln_free_D = emit_ffn(sl * 4096, lnv, wbuf, fst1, hA, hB, X)

        otr_base = s_ev.v

        @block.tensor
        def _(t, sl=sl, val=ffn_done, ob=otr_base):
            t.wait_ge(s_ev.h, val)
            for tt in range(4):
                for half in range(2):
                    k = tt * 2 + half
                    pt = (pA, pB, pE, pF)[k % 4]
                    bwait(t, pt)
                    if k >= 4:
                        t.wait_ge(s_ev.h, ob + k - 3)
                    for j in range(4):
                        sig = half * 4 + j
                        mm = t.transpose(
                            pt.ap().rearrange("p (j q) -> p j q", j=4)[
                                :, j, :],
                            X[:, sl * 4096 + sig * TC + tt * 128:
                              sl * 4096 + sig * TC + tt * 128 + 128],
                            s_eye[:, :])
                    s_mm.inc(mm)
        otr_mms = [s_mm.v - 8 + i + 1 for i in range(8)]

        @block.vector
        def _(v, sl=sl, vals=otr_mms):
            k = 0
            for tt in range(4):
                for half in range(2):
                    pt = (pA, pB, pE, pF)[(tt * 2 + half) % 4]
                    v.wait_ge(s_mm.h, vals[k])
                    s_ev.inc(v.tensor_copy(
                        out=souttm[:, tt * 1024 + half * 512:
                                   tt * 1024 + (half + 1) * 512],
                        in_=pt[:, :]))
                    k += 1
        oev = [s_ev.v - 8 + i + 1 for i in range(8)]
        for i, pt in enumerate([pA, pB, pE, pF]):
            bdone(pt, s_ev, oev[4 + i])

        @block.sync
        def _(s, sl=sl, val=oev[-1]):
            s.wait_ge(s_ev.h, val)
            s_st.inc(s.dma_start(
                out=out_tok.rearrange("(sl tt p) d -> sl p tt d", sl=4,
                                      tt=4)[sl, :, :, :],
                in_=souttm.ap().rearrange("p (tt d) -> p tt d", tt=4)), 16)
        prev_store = s_st.v

    @block.sync
    def _(s):
        s_st.inc(s.dma_start(out=out_id[:, :], in_=sid[:, :]), 16)
        s.wait_ge(s_st.h, s_st.v)

    scD.close()
    es.close()
    return nc


def kernel(**inputs):
    ins = {k: np.asarray(v) for k, v in inputs.items()}
    pre = {
        "x0": np.ascontiguousarray(ins["L0_attn_a_hidden_states"], np.float32),
        "W_in": np.ascontiguousarray(ins["L0_attn_a_W_in"], np.float32),
        "conv_w": np.ascontiguousarray(
            ins["L0_attn_a_conv_w"].T.reshape(2, 8, 128, 4)
            .transpose(0, 2, 1, 3).reshape(2, 128, 32), np.float32),
        "conv_b": np.ascontiguousarray(
            ins["L0_attn_a_conv_b"].reshape(2, 8, 128)
            .transpose(0, 2, 1), np.float32),
        "W_x": np.ascontiguousarray(ins["L0_attn_a_W_x"], np.float32),
        "W_dt": np.ascontiguousarray(ins["L0_attn_a_W_dt"], np.float32),
        "b_dt": np.ascontiguousarray(
            ins["L0_attn_a_b_dt"].reshape(2, 8, 128)
            .transpose(0, 2, 1), np.float32),
        "A_log": np.ascontiguousarray(
            ins["L0_attn_a_A_log"].reshape(2, 8, 128, 16)
            .transpose(0, 2, 1, 3).reshape(2, 128, 128), np.float32),
        "D_vec": np.ascontiguousarray(
            ins["L0_attn_a_D"].reshape(2, 8, 128)
            .transpose(0, 2, 1), np.float32),
        "W_out": np.ascontiguousarray(ins["L0_attn_a_W_out"], np.float32),
        "W_spinT": np.ascontiguousarray(ins["L0_attn_b_W_spin"].T, np.float32),
        "b_spin": np.ascontiguousarray(
            ins["L0_attn_b_b_spin"].reshape(8, 128).T, np.float32),
        "W_globT": np.ascontiguousarray(ins["L0_attn_b_W_global"].T,
                                        np.float32),
        "W_JT": np.ascontiguousarray(ins["L0_attn_b_W_J"].T, np.float32),
        "beta": np.ascontiguousarray(
            ins["L0_attn_b_beta"].reshape(1, 1), np.float32),
        "w1T": np.ascontiguousarray(ins["L0_ffn_w1_w"].T, np.float32),
        "w1s": np.ascontiguousarray(
            ins["L0_ffn_w1_s"].reshape(32, 128).T, np.float32),
        "w2T": np.ascontiguousarray(ins["L0_ffn_w2_w"].T, np.float32),
        "w2s": np.ascontiguousarray(
            ins["L0_ffn_w2_s"].reshape(32, 128).T, np.float32),
        "h1T": np.ascontiguousarray(ins["L0_ffn_h1_w"].T, np.float32),
        "h1s": np.ascontiguousarray(
            ins["L0_ffn_h1_s"].reshape(32, 128).T, np.float32),
        "h2T": np.ascontiguousarray(ins["L0_ffn_h2_w"].T, np.float32),
        "h2s": np.ascontiguousarray(
            ins["L0_ffn_h2_s"].reshape(32, 128).T, np.float32),
        "w3T": np.ascontiguousarray(ins["L0_ffn_w3_w"].T, np.float32),
        "w3s": np.ascontiguousarray(
            ins["L0_ffn_w3_s"].reshape(8, 128).T, np.float32),
        "eye": np.eye(128, dtype=np.float32),
        "constv": np.ascontiguousarray(
            np.tile(np.array([1e-5, 1.0, -0.5, -1.0, 1.0 / 32.0, 0.5, 0.0,
                              2.0], np.float32), (128, 1))),
    }
    nc = build_program()
    res = run_bass_kernel_spmd(nc, [dict(pre) for _ in range(8)],
                               core_ids=list(range(8)))
    global LAST_RESULTS
    LAST_RESULTS = res.results
    out = np.zeros((NB, T, D), np.float32)
    for c in range(8):
        r = res.results[c]
        ids = r["out_id"][0]
        b, par = int(ids[0]), int(ids[1])
        out[b, par * 2048:(par + 1) * 2048, :] = r["out_tok"]
    return out
